# version 05 (final)
# baseline (speedup 1.0000x reference)
import contextlib
import numpy as np
import ml_dtypes
import concourse.bass as bass
import concourse.mybir as mybir
from concourse.bass_utils import run_bass_kernel_spmd

F32 = mybir.dt.float32
F32R = mybir.dt.float32r
BF16 = mybir.dt.bfloat16
AF = mybir.ActivationFunctionType
ALU = mybir.AluOpType
AX = mybir.AxisListType
BF = ml_dtypes.bfloat16

ENGINES = ["sync", "gpsimd", "tensor", "vector", "scalar"]
ENG_MAP = {"pool": "gpsimd"}

B, L, D = 8, 8192, 1024
H, DH = 4, 256
NE, DFF = 8, 4096
TB = 1024
NCORES = 8

CFG = {"nblk": L // TB, "ne": NE, "use_s": True, "use_attn": True}


AUTO_CHAIN = {"vector": "dv", "scalar": "act", "pool": "gp"}


class Sched:

    def __init__(self, cnt=None):
        self.ops = []
        self.cnt = cnt if cnt is not None else {}

    def add(self, engine, fn, waits=(), sem=None, inc=0):
        waits = [w for w in waits if w is not None]
        if engine in AUTO_CHAIN:
            csem = AUTO_CHAIN[engine]
            waits.append((csem, self.cnt.get(csem, 0)))
            if inc == 0:
                sem, inc = csem, 1
        tok = None
        if inc:
            assert sem is not None
            self.cnt[sem] = self.cnt.get(sem, 0) + inc
            tok = (sem, self.cnt[sem])
        self.ops.append((engine, fn, waits, sem, inc))
        return tok

    def tok(self, sem):
        return (sem, self.cnt.get(sem, 0))

    def emit(self, block, sems):
        for engname in ENGINES:
            ops = [o for o in self.ops if ENG_MAP.get(o[0], o[0]) == engname]
            if not ops:
                continue

            def body(eng, ops=ops):
                last_waited = {}
                for _, fn, waits, sem, inc in ops:
                    for (s, v) in waits:
                        if v <= 0 or last_waited.get(s, -1) >= v:
                            continue
                        eng.wait_ge(sems[s], v)
                        last_waited[s] = v
                    inst = fn(eng)
                    if inc:
                        inst.then_inc(sems[sem], inc)
            getattr(block, engname)(body)


def MM(out, lhsT, rhs, start, stop):
    return lambda e: e.matmul(out, lhsT, rhs, start=start, stop=stop)

def TRANS(out, in_, ident):
    def f(e):
        e.transpose(out, in_, ident)
        return e.drain()
    return f


def MMD(out, lhsT, rhs, start, stop):
    def f(e):
        e.matmul(out, lhsT, rhs, start=start, stop=stop)
        return e.drain()
    return f

def DMA(out, in_):
    return lambda e: e.dma_start(out=out, in_=in_)

def TT(out, a, b, op):
    return lambda e: e.tensor_tensor(out=out, in0=a, in1=b, op=op)

def TS(out, in0, s1, s2, op0, op1=None):
    if op1 is None:
        return lambda e: e.tensor_scalar(out=out, in0=in0, scalar1=s1, scalar2=None, op0=op0)
    return lambda e: e.tensor_scalar(out=out, in0=in0, scalar1=s1, scalar2=s2, op0=op0, op1=op1)

def STT(out, in0, s, in1, op0, op1):
    return lambda e: e.scalar_tensor_tensor(out=out, in0=in0, scalar=s, in1=in1, op0=op0, op1=op1)

def ACTF(out, in_, func, bias=0.0, scale=1.0):
    return lambda e: e.activation(out, in_, func, bias=bias, scale=scale)


def ACTD(out, in_, func, bias=0.0, scale=1.0):
    def f(e):
        e.activation(out, in_, func, bias=bias, scale=scale)
        return e.drain()
    return f

def COPY(out, in_):
    return lambda e: e.tensor_copy(out, in_)

def RED(out, in_, op, axis=AX.X):
    return lambda e: e.tensor_reduce(out=out, in_=in_, axis=axis, op=op)

def RECIP(out, in_):
    return lambda e: e.reciprocal(out, in_)

def NOP():
    return lambda e: e.nop()


class MoeCtx:
    def __init__(self):
        self.gate_free = {}
        self.up_free = {}
        self.down_free = {}
        self.pb_free = None
        self.gu_free = {}
        self.wd_free = {}
        self.acc_free = {}
        self.out_done = {}
        self.sg_free = {0: None, 1: None}
        self.all_mm_done_dv = None
        self.prev_act = None
        self.comb_last = None
        self.cwt_free = None
        self.cw_free = None


def emit_moe(S, dx, d_out, d_s, dws, sb, psb, cn, use_s, nblk, ne, parts="all"):
    C = MoeCtx()
    s_xn, s_xnup, s_h, s_acc = sb["xn"], sb["xnup"], sb["h"], sb["acc"]
    s_wg, s_wu, s_wd, s_sg = sb["wg"], sb["wu"], sb["wd"], sb["sg"]
    s_rw, s_stat, s_statr = sb["rw"], sb["stat"], sb["statr"]
    s_ltm, s_top, s_cw = sb["ltm"], sb["top"], sb["cw"]
    s_xbf = s_xnup[:, 0:2 * TB]
    s_xsq = s_xnup[:, 2 * TB:4 * TB]
    s_outb = s_xnup[:, 4 * TB:6 * TB].bitcast(F32)
    p_g, p_u, p_d, p_b = psb["g"], psb["u"], psb["d"], psb["b"]

    x_blk = s_h.bitcast(F32)[:, 0:8 * TB]
    xv = s_h.bitcast(F32)[:, 8 * TB:12 * TB]
    sv = s_h.bitcast(F32)[:, 12 * TB:16 * TB]

    for blk in range(nblk):
        t0 = blk * TB
        t_x = S.add("gpsimd", DMA(x_blk.rearrange("p (k t) -> p k t", k=8),
                                  dx.rearrange("(k p) t -> p k t", p=128)[:, :, t0:t0 + TB]),
                    waits=[C.all_mm_done_dv], sem="xa", inc=16)
        tok_k = {}
        tok_ka = {}
        t_sum_done = None
        for k in range(8):
            par = k % 2
            xb = s_xbf[:, par * TB:(par + 1) * TB]
            xq = s_xsq[:, par * TB:(par + 1) * TB]
            xk = x_blk[:, k * TB:(k + 1) * TB]
            t_bf = S.add("vector", COPY(xb, xk),
                         waits=[t_x, tok_k.get(k - 2), tok_ka.get(k - 2)], sem="dv", inc=1)
            tok_ka[k] = S.add("scalar", ACTF(xq, xb, AF.Square),
                              waits=[t_bf, tok_k.get(k - 2)], sem="act", inc=1)
            for s in range(2):
                sl = slice(s * 512, (s + 1) * 512)
                S.add("tensor", MM(p_d[0:1, s * 512:(s + 1) * 512], cn["ones_bf"], xb[:, sl], k == 0, k == 7),
                      waits=[t_bf])
                tok_k[k] = S.add("tensor", MM(p_d[32:33, s * 512:(s + 1) * 512], cn["ones_bf"], xq[:, sl], k == 0, k == 7),
                                 waits=[tok_ka[k]], sem="mm", inc=1)
        t_stats_mm = tok_k[7]
        st0 = s_stat[0:1, :]
        rrow = s_statr[0:1, :]
        t_mu = S.add("vector", TS(st0, p_d[0:1, 0:TB], 1.0 / D, None, ALU.mult),
                     waits=[t_stats_mm], sem="dv", inc=1)
        t_mur = S.add("vector", COPY(rrow, st0), waits=[t_mu], sem="dv", inc=1)
        t_xn_last = None
        xn_scr = s_sg[:, 0:512]
        t_bm = {}
        for s in range(2):
            sl = slice(s * 512, (s + 1) * 512)
            t_bm[s] = S.add("tensor", MM(p_b[:, sl], cn["ones_row"], rrow[:, sl], True, True),
                            waits=[t_mur, C.pb_free], sem="mm", inc=1)
        st1 = s_xnup[:, 6 * TB:8 * TB].bitcast(F32)[0:1, 0:TB]
        S.add("vector", TT(st1, st0, st0, ALU.mult), waits=[t_mur])
        S.add("vector", STT(st0, p_d[32:33, 0:TB], 1.0 / D, st1, ALU.mult, ALU.subtract),
              waits=[t_stats_mm])
        S.add("vector", TS(st1, st0, 1e-5, None, ALU.add))
        t_inv = S.add("vector", RECIP(st0, st1), sem="dv", inc=1)
        t_rsr = S.add("scalar", ACTD(rrow, st0, AF.Sqrt), waits=[t_inv, t_bm[0], t_bm[1]],
                      sem="act", inc=1)
        for s in range(2):
            sl = slice(s * 512, (s + 1) * 512)
            t_br = S.add("tensor", MM(p_u[:, sl], cn["ones_row"], rrow[:, sl], True, True),
                         waits=[t_rsr, C.up_free.get(s)], sem="mm", inc=1)
            for k in range(8):
                xk = x_blk[:, k * TB + s * 512: k * TB + (s + 1) * 512]
                S.add("vector", TT(xn_scr, xk, p_b[:, sl], ALU.subtract),
                      waits=[t_bm[s], C.sg_free[0]])
                t_xn_last = S.add("vector", TT(s_xn[:, k * TB + s * 512: k * TB + (s + 1) * 512],
                                               xn_scr, p_u[:, sl], ALU.mult),
                                  waits=[t_br], sem="dv", inc=1)
        C.pb_free = t_xn_last
        C.sg_free[0] = t_xn_last
        C.up_free[0] = t_xn_last; C.up_free[1] = t_xn_last
        if parts == "ln":
            C.all_mm_done_dv = t_xn_last
            continue
        t_lg = None
        for tc in range(8):
            for k in range(8):
                t_lg = S.add("tensor", MM(p_g[:, tc * 8:(tc + 1) * 8],
                                          s_xn[:, k * TB + tc * 128: k * TB + tc * 128 + 128],
                                          s_rw[:, k * 8:(k + 1) * 8],
                                          k == 0, k == 7),
                             waits=([t_xn_last, C.gate_free.get(0), C.gate_free.get(1)]
                                    if (tc == 0 and k == 0) else ()),
                             sem="mm", inc=(1 if k == 7 else 0))
        t_ltm = S.add("vector", COPY(s_ltm, p_g[:, 0:64]), waits=[t_lg], sem="dv", inc=1)
        C.gate_free[0] = t_ltm
        if parts == "rtr" or parts == "ltm":
            C.all_mm_done_dv = t_ltm
            continue
        lt3 = s_ltm.rearrange("p (c e) -> p c e", e=8)
        def col3(ap):
            return ap.rearrange("p (c one) -> p c one", one=1)
        m1 = s_top[:, 0:8]; r1 = s_top[:, 8:16]; e1 = s_top[:, 16:24]
        m2 = s_top[:, 24:32]; r2 = s_top[:, 32:40]; e2 = s_top[:, 40:48]
        dd = s_top[:, 48:56]; w1 = s_top[:, 56:64]; w2 = s_top[:, 64:72]
        eq = s_top[:, 128:192]; eqb = s_top[:, 192:256]
        eq3 = eq.rearrange("p (c e) -> p c e", e=8)
        eqb3 = eqb.rearrange("p (c e) -> p c e", e=8)
        msk1 = sb["msk1"]; msk2 = sb["msk2"]; mskd = sb["mskd"]; t1m = sb["t1m"]; cwtm = sb["cwtm"]
        m13 = col3(m1); m23 = col3(m2)
        io3 = cn["iota8"].rearrange("p (c e) -> p c e", e=8)
        a = S.add
        a("vector", RED(m13, lt3, ALU.max), waits=[t_ltm])
        a("vector", TT(eq3, lt3, m13.broadcast_to([128, 8, 8]), ALU.is_ge))
        a("vector", TT(eqb, eq, cn["iota8r"], ALU.mult))
        a("vector", RED(col3(r1), eqb3, ALU.max))
        a("vector", TS(e1, r1, -1.0, 7.0, ALU.mult, ALU.add))
        a("vector", TT(msk1.rearrange("p (c e) -> p c e", e=8), io3,
                       col3(e1).broadcast_to([128, 8, 8]), ALU.is_equal))
        a("vector", STT(mskd, msk1, -1e30, s_ltm, ALU.mult, ALU.add))
        a("vector", RED(m23, mskd.rearrange("p (c e) -> p c e", e=8), ALU.max))
        a("vector", TT(eq3, mskd.rearrange("p (c e) -> p c e", e=8),
                       m23.broadcast_to([128, 8, 8]), ALU.is_ge))
        a("vector", TT(eqb, eq, cn["iota8r"], ALU.mult))
        a("vector", RED(col3(r2), eqb3, ALU.max))
        a("vector", TS(e2, r2, -1.0, 7.0, ALU.mult, ALU.add))
        a("vector", TT(msk2.rearrange("p (c e) -> p c e", e=8), io3,
                       col3(e2).broadcast_to([128, 8, 8]), ALU.is_equal))
        t_dd = a("vector", TT(dd, m1, m2, ALU.subtract), sem="dv", inc=1)
        t_w1 = a("scalar", ACTF(w1, dd, AF.Sigmoid, scale=1.0), waits=[t_dd], sem="act", inc=1)
        t_w2 = a("scalar", ACTF(w2, dd, AF.Sigmoid, scale=-1.0), waits=[t_dd], sem="act", inc=1)
        a("vector", TT(t1m.rearrange("p (c e) -> p c e", e=8), msk1.rearrange("p (c e) -> p c e", e=8),
                       col3(w1).broadcast_to([128, 8, 8]), ALU.mult), waits=[t_w1])
        a("vector", TT(cwtm.rearrange("p (c e) -> p c e", e=8), msk2.rearrange("p (c e) -> p c e", e=8),
                       col3(w2).broadcast_to([128, 8, 8]), ALU.mult), waits=[t_w2])
        t_cwtm = a("vector", TT(sb["cwr"], cwtm, t1m, ALU.add), sem="dv", inc=1)
        if parts == "top":
            C.all_mm_done_dv = t_cwtm
            continue
        p_cwt = p_b.bitcast(BF16)[0:64, 0:128]
        t_tr = S.add("tensor", TRANS(p_cwt, sb["cwr"], cn["id_bf"]),
                     waits=[t_cwtm], sem="mm", inc=1)
        t_cwe = S.add("vector", COPY(sb["cwT"], p_cwt), waits=[t_tr], sem="dv", inc=1)
        C.pb_free = t_cwe
        t_cw1 = S.add("sync", DMA(dws["cwscr"], sb["cwT"]),
                      waits=[t_cwe, C.cw_free], sem="cwd", inc=16)
        t_cw = S.add("sync", DMA(s_cw.rearrange("e (c t) -> e c t", t=128),
                                 dws["cwscr"].rearrange("(c e) t -> e c t", e=8)),
                     waits=[t_cw1], sem="cwd", inc=16)

        if parts == "pre":
            C.all_mm_done_dv = t_cw
            continue
        GW = 4 * 128
        NG = 8
        acc_tok = {}
        for e in range(ne):
            t_bc = []
            for s in range(2):
                t_bc.append(S.add("tensor", MM(p_b[:, s * 512:(s + 1) * 512],
                                               cn["E8"][:, e * 128:(e + 1) * 128],
                                               s_cw[:, s * 512:(s + 1) * 512], True, True),
                                  waits=[t_cw, C.pb_free, C.out_done.get((0, 0)),
                                         C.out_done.get((1, 0))], sem="mm", inc=1))
            t_xnup = None
            for k in range(8):
                for s in range(2):
                    t_xnup = S.add("vector", TT(
                        s_xnup[:, k * TB + s * 512: k * TB + (s + 1) * 512],
                        s_xn[:, k * TB + s * 512: k * TB + (s + 1) * 512],
                        p_b[:, s * 512:(s + 1) * 512], ALU.mult),
                        waits=[t_xn_last, t_bc[s]], sem="dv", inc=1)
            C.pb_free = t_xnup
            C.cw_free = t_bc[1]

            gu_dma = {}
            def ld_gu(g):
                par = g % 2
                tg = S.add("sync", DMA(s_wg[:, par * GW * 8:(par + 1) * GW * 8],
                                       dws["wgu"][e, 0, g].rearrange("p k i f -> p (k i f)")),
                           waits=[C.gu_free.get(par)], sem="wg%d" % par, inc=16)
                tu = S.add("sync", DMA(s_wu[:, par * GW * 8:(par + 1) * GW * 8],
                                       dws["wgu"][e, 1, g].rearrange("p k i f -> p (k i f)")),
                           sem="wu%d" % par, inc=16)
                gu_dma[g] = (tg, tu)
            wd_dma = {}
            def ld_wd(j):
                par = j % 2
                wd_dma[j] = S.add("sync", DMA(s_wd[:, par * 4096:(par + 1) * 4096],
                                              dws["wd"][e, j].rearrange("p i c -> p (i c)")),
                                  waits=[C.wd_free.get(par)], sem="wd%d" % par, inc=16)
            ld_gu(0); ld_gu(1)
            ld_wd(0); ld_wd(1)


            h_last = None
            for g in range(NG):
                par = g % 2
                base = par * GW * 8
                for ii in range(4):
                    i = g * 4 + ii
                    ph = i % 2
                    for s in range(2):
                        pg_h = p_g[:, ph * 512:(ph + 1) * 512]
                        pu_h = p_u[:, ph * 512:(ph + 1) * 512]
                        gate_w = [gu_dma[g][0], C.gate_free.get(ph), t_xn_last]
                        up_w = [gu_dma[g][1], C.up_free.get(ph), t_xnup]
                        tg = tu = None
                        for k in range(8):
                            wsl = s_wg[:, base + (k * 4 + ii) * 128: base + (k * 4 + ii) * 128 + 128]
                            tg = S.add("tensor", MM(pg_h, wsl,
                                                    s_xn[:, k * TB + s * 512: k * TB + (s + 1) * 512],
                                                    k == 0, k == 7),
                                       waits=(gate_w if k == 0 else ()), sem="mm", inc=(1 if k == 7 else 0))
                        for k in range(8):
                            wsl = s_wu[:, base + (k * 4 + ii) * 128: base + (k * 4 + ii) * 128 + 128]
                            tu = S.add("tensor", MM(pu_h, wsl,
                                                    s_xnup[:, k * TB + s * 512: k * TB + (s + 1) * 512],
                                                    k == 0, k == 7),
                                       waits=(up_w if k == 0 else ()), sem="mm", inc=(1 if k == 7 else 0))
                        sgh = s_sg[:, s * 512:(s + 1) * 512]
                        te = S.add("scalar", ACTF(sgh, pg_h, AF.Silu),
                                   waits=[tg, C.sg_free[s]], sem="act", inc=1)
                        C.gate_free[ph] = te
                        th = S.add("vector", TT(s_h[:, i * TB + s * 512: i * TB + (s + 1) * 512],
                                                sgh, pu_h, ALU.mult),
                                   waits=[te, tu], sem="dv", inc=1)
                        C.up_free[ph] = th
                        C.sg_free[s] = th
                        h_last = th
                C.gu_free[par] = S.tok("mm")
                if g + 2 < NG:
                    ld_gu(g + 2)
            for j in range(8):
                par = j % 2
                for s in range(2):
                    ph = s
                    pd_h = p_d[:, ph * 512:(ph + 1) * 512]
                    pre = [wd_dma[j], h_last, C.down_free.get(ph)]
                    td = None
                    for i in range(32):
                        wsl = s_wd[:, par * 4096 + i * 128: par * 4096 + i * 128 + 128]
                        td = S.add("tensor", MM(pd_h, wsl,
                                                s_h[:, i * TB + s * 512: i * TB + (s + 1) * 512],
                                                i == 0, i == 31),
                                   waits=(pre if i == 0 else ()), sem="mm", inc=(1 if i == 31 else 0))
                    accsl = s_acc[:, j * TB + s * 512: j * TB + (s + 1) * 512]
                    if e == 0:
                        ta = S.add("vector", COPY(accsl, pd_h),
                                   waits=[td, C.acc_free.get((j, s))], sem="dv", inc=1)
                    else:
                        ta = S.add("vector", TT(accsl, accsl, pd_h, ALU.add),
                                   waits=[td], sem="dv", inc=1)
                    C.down_free[ph] = ta
                    acc_tok[(j, s)] = ta
                C.wd_free[par] = S.tok("mm")
                if j + 2 < 8:
                    ld_wd(j + 2)

        C.prev_act = S.tok("act")
        all_done = acc_tok[(7, 1)]
        if parts == "exp":
            C.all_mm_done_dv = all_done
            continue
        prev_comb = None
        for s in range(2):
            t_xl = S.add("gpsimd", DMA(xv.rearrange("p (k t) -> p k t", k=8),
                                       dx.rearrange("(k p) t -> p k t", p=128)[:, :, t0 + s * 512:t0 + (s + 1) * 512]),
                         waits=[all_done, prev_comb], sem="xv", inc=16)
            t_sl = None
            if use_s:
                t_sl = S.add("gpsimd", DMA(sv.rearrange("p (k t) -> p k t", k=8),
                                           d_s[:, :, t0 + s * 512:t0 + (s + 1) * 512].rearrange("k p t -> p k t")),
                             waits=[all_done, prev_comb], sem="sv", inc=16)
            for j in range(8):
                accsl = s_acc[:, j * TB + s * 512: j * TB + (s + 1) * 512]
                ga = s_sg[:, 0:512]; gb = s_sg[:, 512:1024]
                tg_ = S.add("scalar", ACTF(ga, accsl, AF.Silu),
                            waits=[acc_tok[(j, s)], C.sg_free[0], prev_comb], sem="act", inc=1)
                C.acc_free[(j, s)] = tg_
                if use_s:
                    tm_ = S.add("vector", TT(gb, ga, sv[:, j * 512:(j + 1) * 512], ALU.mult),
                                waits=[tg_, t_sl, C.sg_free[1]], sem="dv", inc=1)
                    src = gb
                else:
                    tm_ = tg_
                    src = ga
                oslot = s_outb[:, (j % 2) * 512:((j % 2) + 1) * 512]
                to_ = S.add("vector", TT(oslot, src, xv[:, j * 512:(j + 1) * 512], ALU.add),
                            waits=[tm_, t_xl, C.out_done.get(((j % 2), 0))], sem="dv", inc=1)
                C.sg_free[0] = to_
                C.sg_free[1] = to_
                prev_comb = to_
                C.out_done[((j % 2), 0)] = S.add("scalar", DMA(d_out[j, :, t0 + s * 512:t0 + (s + 1) * 512],
                                                               oslot),
                                                 waits=[to_], sem="st%d" % (j % 2), inc=16)
        C.all_mm_done_dv = all_done
    return C




TA = 512
NBLKA = L // TA


def emit_attn(S, dx, d_s, dwa, sb, ps, cn, nblka=NBLKA, dbg=None, aparts="all"):
    a_x = sb["x"]; a_xn = sb["xn"]; a_z = sb["z"]
    a_cs = sb["cs"]; a_csp = sb["csp"]
    a_q = sb["q"]; a_k = sb["k"]; a_v = sb["v"]
    a_lo = sb["lo"]; a_mi = sb["mi"]
    otm = sb["otm"]
    a_feat = sb["feat"]; a_fT = sb["fT"]; a_ptm = sb["ptm"]
    a_h1 = sb["h1"]; a_mx = sb["mx"]; a_on = sb["on"]; a_onf = sb["onf"]
    a_sout = sb["z"]; a_stat = sb["stat"]; a_statr = sb["statr"]
    a_mb = sb["mb"]
    a_tl = sb["tl"]
    ch = sb["ch"]; a_sc = sb["sc"]; a_sbf = sb["sbf"]; a_dt = sb["dt"]
    a_dt2 = sb["dt2"]; a_xm = sb["xm"]; a_row = sb["row"]; a_sc2 = sb["sc2"]
    beta_tm = sb["beta"]
    arena = sb["arena"]
    rw2 = sb["rw2"]; wb_r = sb["wb"]; cvw = sb["cvw"]
    rb1 = sb["rb1"]; rb2r = sb["rb2"]; mixw = sb["mixw"]; onormw = sb["onorm"]

    a_sf = sb["sf32"]
    p_a, p_b2, p_c, p_d2 = ps["a"], ps["b2"], ps["c"], ps["d2"]

    CW_OFF = {"q": 0, "k": 4, "v": 8, "lo": 12, "mi": 19}

    X = {}

    def chsl(i):
        return ch[:, i * 512:(i + 1) * 512]

    t_ms = S.add("vector", (lambda e: e.memset(a_sbf, 0.0)), sem="dv", inc=1)
    for h in range(4):
        X[("sbf", h)] = t_ms
    S.add("vector", (lambda e: e.memset(a_tl, 0.0)), sem="dv", inc=1)
    S.add("vector", (lambda e: e.memset(a_feat, 0.0)), sem="dv", inc=1)
    S.add("vector", (lambda e: e.memset(a_sc2, 0.0)), sem="dv", inc=1)

    for blk in range(nblka):
        t0 = blk * TA
        wtok = {}

        def ld_w(m, extra=()):
            par = m % 2
            wtok[m] = S.add("sync", DMA(arena[:, par * 8192:(par + 1) * 8192].rearrange("p (k f) -> p k f", k=8),
                                        dwa["wqkv"][m]),
                            waits=[X.get(("wfree", par))] + list(extra), sem="wq%d" % par, inc=16)
        ld_w(0); ld_w(1)

        t_x = S.add("gpsimd", DMA(a_x.rearrange("p (k t) -> p k t", k=8),
                                  dx.rearrange("(k p) t -> p k t", p=128)[:, :, t0:t0 + TA]),
                    waits=[X.get("xfree")], sem="xa", inc=16)
        tok_k = {}; tok_ka = {}
        for k in range(8):
            par = k % 2
            xq = a_csp.bitcast(BF16)[:, par * TA:(par + 1) * TA]
            xk = a_x[:, k * TA:(k + 1) * TA]
            tok_ka[k] = S.add("scalar", ACTD(xq, xk, AF.Square),
                              waits=[t_x, tok_k.get(k - 2)] +
                                    ([X.get(("scanfree", 2)), X.get(("scanfree", 3))] if k == 0 else []),
                              sem="act", inc=1)
            S.add("tensor", MM(p_a[0:1, 0:TA], cn["ones_bf"], xk, k == 0, k == 7),
                  waits=([t_x, X.get("pa_free")] if k == 0 else []))
            tok_k[k] = S.add("tensor", MM(p_a[32:33, 0:TA], cn["ones_bf"], xq, k == 0, k == 7),
                             waits=[tok_ka[k]], sem="mm", inc=1)
        mu_ = a_stat[0:1, 0:TA]; var_ = a_stat[0:1, TA:2 * TA]
        rrow = a_statr[0:1, 0:TA]
        t_mu = S.add("vector", TS(mu_, p_a[0:1, 0:TA], 1.0 / D, None, ALU.mult),
                     waits=[tok_k[7], X.get("statact")], sem="dv", inc=1)
        t_mur = S.add("vector", COPY(rrow, mu_), waits=[t_mu], sem="dv", inc=1)
        t_bm = S.add("tensor", MM(p_b2[:, 0:TA], cn["ones_row"], rrow, True, True),
                     waits=[t_mur, X.get("pb2_free")], sem="mm", inc=1)
        S.add("vector", TT(var_, mu_, mu_, ALU.mult), waits=[t_mur])
        S.add("vector", STT(mu_, p_a[32:33, 0:TA], 1.0 / D, var_, ALU.mult, ALU.subtract))
        S.add("vector", TS(var_, mu_, 1e-5, None, ALU.add))
        t_inv = S.add("vector", RECIP(mu_, var_), sem="dv", inc=1)
        X["pa_free"] = t_inv
        t_rsr = S.add("scalar", ACTD(rrow, mu_, AF.Sqrt), waits=[t_inv, t_bm], sem="act", inc=1)
        X["statact"] = t_rsr
        t_br = S.add("tensor", MM(p_c[:, 0:TA], cn["ones_row"], rrow, True, True),
                     waits=[t_rsr, X.get("pc_free")], sem="mm", inc=1)
        t_xn = None
        for k in range(8):
            scr = a_sc[:, 0:TA]
            S.add("vector", TT(scr, a_x[:, k * TA:(k + 1) * TA], p_b2[:, 0:TA], ALU.subtract),
                  waits=[t_bm, X.get("scact")])
            t_xn = S.add("vector", TT(a_xn[:, k * TA:(k + 1) * TA], scr, p_c[:, 0:TA], ALU.mult),
                         waits=[t_br], sem="dv", inc=1)
        X["xfree"] = t_xn
        X["pb2_free"] = t_xn
        X["pc_free"] = t_xn

        if aparts == "ln":
            continue
        t_bmm = None
        for tc in range(4):
            for k in range(8):
                t_bmm = S.add("tensor", MM(p_d2[:, tc * 4:(tc + 1) * 4],
                                           a_xn[:, k * TA + tc * 128: k * TA + tc * 128 + 128],
                                           wb_r[:, k * 4:(k + 1) * 4], k == 0, k == 7),
                              waits=([t_xn, X.get("pd2_free")] if (tc == 0 and k == 0) else ()),
                              sem="mm", inc=(1 if k == 7 else 0))
        t_beta = S.add("scalar", ACTF(beta_tm, p_d2[:, 0:16], AF.Sigmoid), waits=[t_bmm], sem="act", inc=1)
        X["pd2_free"] = t_beta

        outdst = {0: a_q, 1: a_k, 2: a_v}
        for m, nm in [(0, "q"), (1, "k"), (2, "v")]:
            for c in range(8):
                par = c % 2
                t_zm = None
                for k in range(8):
                    wsl = arena[:, (m % 2) * 8192 + k * 1024 + c * 128:(m % 2) * 8192 + k * 1024 + c * 128 + 128]
                    t_zm = S.add("tensor", MM(p_a[:, 0:512], wsl, a_xn[:, k * TA:(k + 1) * TA],
                                              k == 0, k == 7),
                                 waits=([wtok[m], X.get("pa_free"), t_xn] if k == 0 else ()),
                                 sem="mm", inc=(1 if k == 7 else 0))
                zt = a_z[:, par * 515:par * 515 + 515]
                tl = a_tl[:, (m * 8 + c) * 3:(m * 8 + c) * 3 + 3]
                S.add("vector", COPY(zt[:, 0:3], tl),
                      waits=[X.get(("ztl", m, c))] + ([X.get(("sof", 0)), X.get(("sof", 1))]
                                                      if (m == 0 and c == 0) else []))
                t_ze = S.add("vector", COPY(zt[:, 3:515], p_a[:, 0:512]), waits=[t_zm], sem="dv", inc=1)
                X["pa_free"] = t_ze
                t_ts = S.add("vector", COPY(tl, zt[:, 512:515]), sem="dv", inc=1)
                X[("ztl", m, c)] = t_ts
                sA = a_sc[:, 0:512]; sB = a_sc[:, 512:1024]
                w0 = cvw[:, c * 44 + CW_OFF[nm]:c * 44 + CW_OFF[nm] + 1]
                S.add("vector", TS(sA, zt[:, 0:512], w0, None, ALU.mult), waits=[X.get("scact")])
                for tau in range(1, 4):
                    wt = cvw[:, c * 44 + CW_OFF[nm] + tau:c * 44 + CW_OFF[nm] + tau + 1]
                    src, dst = (sA, sB) if tau % 2 == 1 else (sB, sA)
                    last = tau == 3
                    S.add("vector", STT(dst, zt[:, tau:tau + 512], wt, src, ALU.mult, ALU.add),
                          sem="dv", inc=(1 if last else 0))
                t_cv = S.tok("dv")
                off = c * 640 + 128 if m == 2 else c * 512
                t_si = S.add("scalar", ACTD(outdst[m][:, off:off + 512], sB, AF.Silu),
                             waits=[t_cv], sem="act", inc=1)
                X["scact"] = t_si
                X[("qkv", m, c)] = t_si
            X[("wfree", m % 2)] = S.tok("mm")
            if m == 0:
                ld_w(2)

        if aparts == "qkv":
            continue
        vt_tok = {}
        for c in range(8):
            vt = a_v[:, c * 640:c * 640 + 640]
            tl = a_tl[:, 72 + c * 24:72 + c * 24 + 24]
            t1 = S.add("vector", COPY(vt[:, 104:128], tl),
                       waits=[X.get(("vtl", c)), X.get(("qkv", 2, c))], sem="dv", inc=1)
            t2 = S.add("vector", COPY(tl, vt[:, 616:640]), sem="dv", inc=1)
            X[("vtl", c)] = t2
            vt_tok[c] = t1

        for kind, nm, K in [(0, "lo", 7), (1, "mi", 25)]:
            for c in range(8):
                vt = a_v[:, c * 640:c * 640 + 640]
                eng = "vector"
                sA = a_sc[:, 0:512]; sB = a_sc[:, 512:1024]
                wpre = [X.get("scact"), vt_tok[c], X.get(("vtl", c))]
                base = 128 - (K - 1)
                w0 = cvw[:, c * 44 + CW_OFF[nm]:c * 44 + CW_OFF[nm] + 1]
                S.add(eng, TS(sA, vt[:, base:base + 512], w0, None, ALU.mult), waits=wpre)
                for tau in range(1, K):
                    wt = cvw[:, c * 44 + CW_OFF[nm] + tau:c * 44 + CW_OFF[nm] + tau + 1]
                    src, dst = (sA, sB) if tau % 2 == 1 else (sB, sA)
                    S.add(eng, STT(dst, vt[:, base + tau:base + tau + 512], wt, src, ALU.mult, ALU.add))
                fin = sB if (K - 1) % 2 == 1 else sA
                dstt = (a_lo if kind == 0 else a_mi)[:, (c % 2) * 512:(c % 2) * 512 + 512]
                t_f = S.add(eng, COPY(dstt, fin), waits=[X.get(("losl", kind, c % 2))],
                            sem=("dv" if eng == "vector" else "gp"), inc=1)
                if eng == "vector":
                    X["scact"] = t_f
                h, cc = c // 2, c % 2
                for tc in range(4 if aparts != "lomicv" else 0):
                    insl = dstt[:, tc * 128:tc * 128 + 128]
                    slot = tc % 4
                    pt = p_c.bitcast(BF16)[:, slot * 128:slot * 128 + 128]
                    t_t = S.add("tensor", TRANS(pt, insl, cn["id_bf"]),
                                waits=[t_f, X.get("ptrl"), X.get(("ptr", slot)), X.get("pc_free")], sem="mm", inc=1)
                    dsl = otm[:, kind * 4096 + tc * 1024 + h * 256 + cc * 128:
                              kind * 4096 + tc * 1024 + h * 256 + cc * 128 + 128]
                    t_e = S.add("vector", COPY(dsl, pt), waits=[t_t], sem="dv", inc=1)
                    X[("ptr", slot)] = t_e
                    X["ptrl"] = t_e
                    X[("otm", kind, tc, h, cc)] = t_e
                X[("losl", kind, c % 2)] = S.tok("mm")
        for c in range(8):
            h, cc = c // 2, c % 2
            for tc in range(4):
                insl = a_v[:, c * 640 + 128 + tc * 128:c * 640 + 128 + tc * 128 + 128]
                slot = tc % 4
                pt = p_c.bitcast(BF16)[:, slot * 128:slot * 128 + 128]
                t_t = S.add("tensor", TRANS(pt, insl, cn["id_bf"]),
                            waits=[X.get(("qkv", 2, c)), X.get("ptrl"), X.get(("ptr", slot)), X.get("pc_free")],
                            sem="mm", inc=1)
                dsl = otm[:, 3 * 4096 + tc * 1024 + h * 256 + cc * 128:
                          3 * 4096 + tc * 1024 + h * 256 + cc * 128 + 128]
                t_e = S.add("vector", COPY(dsl, pt), waits=[t_t], sem="dv", inc=1)
                X[("ptr", slot)] = t_e
                X["ptrl"] = t_e
                X[("otm", 3, tc, h, cc)] = t_e

        if dbg is not None and blk == 0 and aparts != "lomicv":
            if True:
                wv = [X[("otm", 3, tcx, hx, ccx)] for tcx in range(4) for hx in range(4) for ccx in range(2)]
                S.add("gpsimd", DMA(dbg["vd0"], otm[:, 3 * 4096:4 * 4096]), waits=wv, sem="dbg", inc=16)
                S.add("vector", NOP(), waits=[S.tok("dbg")], sem="dv", inc=1)
        if aparts in ("lomi", "lomicv"):
            continue
        p_e, p_f, p_g2, p_h2 = ps["e"], ps["f"], ps["g2"], ps["h2"]
        scanps = [p_e, p_f, p_g2, p_h2]
        mstsl = lambda g_: chsl(10 + g_ % 2)

        def bc128(ap):
            return ap.rearrange("p (one f) -> p one f", one=1).broadcast_to([128, 4, 128])

        def v3(sl):
            return chsl(sl).rearrange("p (hh f) -> p hh f", f=128)

        def rr(gens):
            gens = list(gens)
            while gens:
                nxt = []
                for gg in gens:
                    try:
                        next(gg)
                        nxt.append(gg)
                    except StopIteration:
                        pass
                gens = nxt

        scl_tok = {}

        def gen_prep(h, g):
            par = h % 2
            ktm = a_dt[:, h * 256:(h + 1) * 256]
            kl2 = a_dt2[:, (g % 2) * 1024 + h * 256:(g % 2) * 1024 + h * 256 + 256]
            scr = a_sc.bitcast(BF16)[:, par * 256:(par + 1) * 256]
            rb = 256 + par * 4
            rkraw = a_sc[:, rb:rb + 1]; rkc = a_sc[:, rb + 1:rb + 2]
            brk = a_sc[:, rb + 2:rb + 3]; rki = a_sc[:, rb + 3:rb + 4]
            qraw = a_sc[0:1, 264 + par * 128:392 + par * 128]
            qinv = a_sc[0:1, 520 + par * 128:648 + par * 128]
            rowb = a_row[0:1, par * 384:par * 384 + 384]
            xmb = a_xm[:, par * 384:par * 384 + 384]
            p_go = p_a if par == 0 else p_b2
            p_sm = p_d2 if par == 0 else p_c
            smo = 0 if par == 0 else 128
            csb = a_cs.bitcast(BF16)[:, par * 256:(par + 1) * 256]
            bcol = beta_tm[:, g * 4 + h:g * 4 + h + 1]
            rqcol = a_sc2[:, 4 + (g % 2) * 4 + h:5 + (g % 2) * 4 + h]
            for cc in range(2):
                insl = a_k[:, (2 * h + cc) * TA + g * 128:(2 * h + cc) * TA + g * 128 + 128]
                pt = p_c.bitcast(BF16)[:, cc * 128:cc * 128 + 128]
                t_t = S.add("tensor", TRANS(pt, insl, cn["id_bf"]),
                            waits=[X.get(("qkv", 1, 2 * h + cc)), X.get("ptrl"), X.get(("ptr", cc))],
                            sem="mm", inc=1)
                tke = S.add("vector", COPY(ktm[:, cc * 128:cc * 128 + 128], pt), waits=[t_t], sem="dv", inc=1)
                X[("ptr", cc)] = tke
                X["ptrl"] = tke
            yield
            S.add("vector", TT(scr, ktm, ktm, ALU.mult), waits=[X.get("scact"), X.get(("scact", par))])
            S.add("vector", RED(rkraw.rearrange("p (a one) -> p a one", one=1),
                                scr.rearrange("p (a x) -> p a x", x=256), ALU.add))
            S.add("vector", TS(rkraw, rkraw, 1e-12, None, ALU.add))
            t_i1 = S.add("vector", RECIP(rki, rkraw), sem="dv", inc=1)
            yield
            t_rk = S.add("scalar", ACTF(rkc, rki, AF.Sqrt), waits=[t_i1], sem="act", inc=1)
            yield
            t_brk = S.add("vector", TT(brk, rkc, bcol, ALU.mult), waits=[t_rk, t_beta], sem="dv", inc=1)
            t_kb = S.add("vector", TS(kl2, ktm, rkc, None, ALU.mult),
                         waits=[t_rk, X.get(("dt2", h))], sem="dv", inc=1)
            scl_tok[h] = t_kb
            t_sqa = {}
            for cc in range(2):
                qs = a_q[:, (2 * h + cc) * TA + g * 128:(2 * h + cc) * TA + g * 128 + 128]
                t_sqa[cc] = S.add("scalar", ACTD(csb[:, cc * 128:cc * 128 + 128], qs, AF.Square),
                                  waits=[X.get(("qkv", 0, 2 * h + cc)), X.get(("csq", par))], sem="act", inc=1)
            yield
            t_qmm = None
            for cc in range(2):
                t_qmm = S.add("tensor", MM(p_sm[0:1, smo:smo + 128], cn["ones_bf"],
                                           csb[:, cc * 128:cc * 128 + 128], cc == 0, cc == 1),
                              waits=([t_sqa[0], t_sqa[1]] +
                                     ([X.get("pd2_free")] if par == 0 else [])
                                     if cc == 0 else ()),
                              sem="mm", inc=(1 if cc else 0))
            t_qrow = S.add("vector", COPY(qraw, p_sm[0:1, smo:smo + 128]), waits=[t_qmm], sem="dv", inc=1)
            X[("csq", par)] = t_qrow
            S.add("vector", TS(qraw, qraw, 1e-12, None, ALU.add))
            t_i2 = S.add("vector", RECIP(qinv, qraw), sem="dv", inc=1)
            yield
            t_rqr = S.add("scalar", ACTF(rowb[0:1, 256:384], qinv, AF.Sqrt), waits=[t_i2], sem="act", inc=1)
            X[("scact", par)] = t_rqr
            yield
            t_rqc = S.add("tensor", MMD(p_sm[:, smo + 128:smo + 129], rowb[0:1, 256:384],
                                        cn["ones_row"][0:1, 0:1], True, True),
                          waits=[t_rqr], sem="mm", inc=1)
            t_rqcol = S.add("vector", COPY(rqcol, p_sm[:, smo + 128:smo + 129]), waits=[t_rqc], sem="dv", inc=1)
            t_ckb = S.add("vector", COPY(a_dt[:, 1024 + par * 2:1025 + par * 2], rkc), sem="dv", inc=1)
            t_ckb2 = S.add("vector", COPY(a_dt[:, 1025 + par * 2:1026 + par * 2], brk), waits=[t_brk],
                           sem="dv", inc=1)
            yield
            t_rkrow = S.add("tensor", MM(p_sm[0:1, smo + 136:smo + 264], a_dt[:, 1024 + par * 2:1025 + par * 2],
                                         cn["id_bf"], True, True),
                            waits=[t_ckb], sem="mm", inc=1)
            t_brkrow = S.add("tensor", MM(p_sm[32:33, smo + 136:smo + 264], a_dt[:, 1025 + par * 2:1026 + par * 2],
                                          cn["id_bf"], True, True),
                             waits=[t_ckb2], sem="mm", inc=1)
            t_r1 = S.add("vector", COPY(rowb[0:1, 0:128], p_sm[0:1, smo + 136:smo + 264]),
                         waits=[t_rkrow], sem="dv", inc=1)
            t_r2 = S.add("vector", COPY(rowb[0:1, 128:256], p_sm[32:33, smo + 136:smo + 264]),
                         waits=[t_brkrow], sem="dv", inc=1)
            if par == 0:
                X["pd2_free"] = t_r2
                X["pd2o"] = t_r2
            else:
                X["pc_free"] = t_r2
            yield
            t_g = None
            for cc in range(2):
                ksl = a_k[:, (2 * h + cc) * TA + g * 128:(2 * h + cc) * TA + g * 128 + 128]
                t_g = S.add("tensor", MM(p_go[:, 0:128], ksl, ksl, cc == 0, cc == 1),
                            waits=([X.get("pa_free") if par == 0 else X.get("pb2_free"),
                                    X.get(("qkv", 1, 2 * h)), X.get(("qkv", 1, 2 * h + 1))]
                                   if cc == 0 else ()),
                            sem="mm", inc=(1 if cc else 0))
            t_kq = None
            for cc in range(2):
                ksl = a_k[:, (2 * h + cc) * TA + g * 128:(2 * h + cc) * TA + g * 128 + 128]
                qsl = a_q[:, (2 * h + cc) * TA + g * 128:(2 * h + cc) * TA + g * 128 + 128]
                t_kq = S.add("tensor", MM(p_go[:, 128:256], ksl, qsl, cc == 0, cc == 1),
                             waits=([X.get(("qkv", 0, 2 * h)), X.get(("qkv", 0, 2 * h + 1))] if cc == 0 else ()),
                             sem="mm", inc=(1 if cc else 0))
            yield
            t_o1 = S.add("tensor", MMD(p_go[:, 256:384], rowb[0:1, 128:256], rowb[0:1, 0:128], True, True),
                         waits=[t_r1, t_r2], sem="mm", inc=1)
            t_x1 = S.add("vector", TT(xmb[:, 0:128], p_go[:, 256:384], cn["mkA"], ALU.mult),
                         waits=[t_o1], sem="dv", inc=1)
            yield
            t_o2 = S.add("tensor", MMD(p_go[:, 384:512], rowb[0:1, 0:128], rowb[0:1, 128:256], True, True),
                         waits=[t_x1], sem="mm", inc=1)
            t_x2 = S.add("vector", TT(xmb[:, 128:256], p_go[:, 384:512], cn["mkAu"], ALU.mult),
                         waits=[t_o2], sem="dv", inc=1)
            yield
            t_o3 = S.add("tensor", MMD(p_go[:, 256:384], rowb[0:1, 0:128], cn["ones_row"], True, True),
                         waits=[t_x2], sem="mm", inc=1)
            t_x3 = S.add("vector", TT(xmb[:, 256:384], p_go[:, 256:384], cn["mkT"], ALU.mult),
                         waits=[t_o3], sem="dv", inc=1)
            if dbg is not None and blk == 0 and g == 0 and h == 0:
                S.add("gpsimd", DMA(dbg["row"], a_row[0:1, 0:384]), waits=[t_r2, t_rqb], sem="dbg", inc=16)
                S.add("gpsimd", DMA(dbg["xm"], a_xm[:, 0:384]), waits=[t_x3], sem="dbg", inc=16)
                S.add("vector", NOP(), waits=[S.tok("dbg")], sem="dv", inc=1)
            yield
            t_A = S.add("vector", TT(chsl(0)[:, h * 128:(h + 1) * 128], p_go[:, 0:128], xmb[:, 0:128], ALU.mult),
                        waits=[t_g, t_x1, X.get("chainmm")], sem="dv", inc=1)
            t_Ab = S.add("vector", TT(chsl(1)[:, h * 128:(h + 1) * 128], p_go[:, 0:128], xmb[:, 128:256],
                                      ALU.mult),
                         waits=[t_x2], sem="dv", inc=1)
            t_aT = S.add("vector", TT(chsl(8 + g % 2)[:, h * 128:(h + 1) * 128], p_go[:, 128:256],
                                      xmb[:, 256:384], ALU.mult),
                         waits=[t_kq, t_x3], sem="dv", inc=1)
            if par == 0:
                X["pa_free"] = t_aT
            else:
                X["pb2_free"] = t_aT
            S.add("vector", TT(chsl(2)[:, h * 128:(h + 1) * 128], chsl(0)[:, h * 128:(h + 1) * 128],
                               cn["id_bf"], ALU.add), waits=[t_A])
            S.add("vector", TT(chsl(3)[:, h * 128:(h + 1) * 128], chsl(1)[:, h * 128:(h + 1) * 128],
                               cn["id_bf"], ALU.add), waits=[t_Ab])

        def gen_chain(g):
            tok_prev = S.tok("dv")
            S.add("vector", TT(v3(4), v3(1), bc128(cn["mD16u"]), ALU.mult), waits=[tok_prev])
            S.add("vector", TT(v3(5), v3(0), bc128(cn["mD16l"]), ALU.mult))
            S.add("vector", TT(v3(3), v3(4), bc128(cn["id_bf"]), ALU.add))
            t_base = S.add("vector", TT(v3(2), v3(5), bc128(cn["id_bf"]), ALU.add), sem="dv", inc=1)
            tok_prev = t_base
            yield
            for j in range(6):
                lvlmask = None if j < 3 else ("mB16", "mB32", "mB64")[j - 3]
                if j < 3:
                    tq = tr = None
                    for h in range(4):
                        tq = S.add("tensor", MM(p_a[:, h * 128:(h + 1) * 128],
                                                chsl(5)[:, h * 128:(h + 1) * 128],
                                                chsl(4)[:, h * 128:(h + 1) * 128], True, True),
                                   waits=([tok_prev, X.get("pa_free")] if h == 0 else ()),
                                   sem="mm", inc=(1 if h == 3 else 0))
                    for h in range(4):
                        tr = S.add("tensor", MM(p_c[:, h * 128:(h + 1) * 128],
                                                chsl(4)[:, h * 128:(h + 1) * 128],
                                                chsl(5)[:, h * 128:(h + 1) * 128], True, True),
                                   waits=([tok_prev, X.get("pc_free")] if h == 0 else ()),
                                   sem="mm", inc=(1 if h == 3 else 0))
                    yield
                    t_q = S.add("vector", COPY(chsl(4), p_a[:, 0:512]), waits=[tq, tr], sem="dv", inc=1)
                    t_r = S.add("vector", COPY(chsl(5), p_c[:, 0:512]), waits=[tr], sem="dv", inc=1)
                else:
                    t_q = S.add("vector", TT(v3(4), v3(1), bc128(cn[lvlmask + "u"]), ALU.mult),
                                waits=[tok_prev], sem="dv", inc=1)
                    t_r = S.add("vector", TT(v3(5), v3(0), bc128(cn[lvlmask + "l"]), ALU.mult),
                                sem="dv", inc=1)
                yield
                ty = tz = None
                for h in range(4):
                    ty = S.add("tensor", MM(p_a[:, h * 128:(h + 1) * 128],
                                            chsl(5)[:, h * 128:(h + 1) * 128],
                                            chsl(3)[:, h * 128:(h + 1) * 128], True, True),
                               waits=([t_q, t_r, X.get("pa_free")] if h == 0 else ()),
                               sem="mm", inc=(1 if h == 3 else 0))
                for h in range(4):
                    tz = S.add("tensor", MM(p_b2[:, h * 128:(h + 1) * 128],
                                            chsl(4)[:, h * 128:(h + 1) * 128],
                                            chsl(2)[:, h * 128:(h + 1) * 128], True, True),
                               waits=([t_q, t_r, X.get("pb2_free")] if h == 0 else ()),
                               sem="mm", inc=(1 if h == 3 else 0))
                yield
                t_y = S.add("vector", COPY(chsl(6), p_a[:, 0:512]), waits=[ty], sem="dv", inc=1)
                t_z = S.add("vector", COPY(chsl(7), p_b2[:, 0:512]), waits=[tz], sem="dv", inc=1)
                yield
                tx = txp = None
                for h in range(4):
                    tx = S.add("tensor", MM(p_a[:, h * 128:(h + 1) * 128],
                                            chsl(2)[:, h * 128:(h + 1) * 128],
                                            chsl(6)[:, h * 128:(h + 1) * 128], True, True),
                               waits=([t_y, t_z] if h == 0 else ()), sem="mm", inc=(1 if h == 3 else 0))
                for h in range(4):
                    txp = S.add("tensor", MM(p_b2[:, h * 128:(h + 1) * 128],
                                             chsl(3)[:, h * 128:(h + 1) * 128],
                                             chsl(7)[:, h * 128:(h + 1) * 128], True, True),
                                waits=([t_y, t_z] if h == 0 else ()), sem="mm", inc=(1 if h == 3 else 0))
                yield
                mdst = mstsl(g) if j == 5 else chsl(3)
                t_m = S.add("vector", TT(mdst, chsl(3), p_a[:, 0:512], ALU.add), waits=[tx, txp],
                            sem="dv", inc=1)
                t_p = S.add("vector", TT(chsl(2), chsl(2), p_b2[:, 0:512], ALU.add), waits=[txp],
                            sem="dv", inc=1)
                X["pa_free"] = t_m
                X["pb2_free"] = t_p
                X["pc_free"] = t_r
                tok_prev = t_p
                yield
            X[("chaindone", g)] = t_m
            if dbg is not None and blk == 0 and g == 0:
                S.add("gpsimd", DMA(dbg["M"], mstsl(0)), waits=[t_m], sem="dbg", inc=16)
                S.add("vector", NOP(), waits=[S.tok("dbg")], sem="dv", inc=1)

        def gen_scan_h(g, h):
            pw = scanps[h]
            kl2 = a_dt2[:, (g % 2) * 1024 + h * 256:(g % 2) * 1024 + h * 256 + 256]
            Msl = mstsl(g)[:, h * 128:(h + 1) * 128]
            aTsl = chsl(8 + g % 2)[:, h * 128:(h + 1) * 128]
            mbs = a_mb[:, h * 128:(h + 1) * 128]
            bcol = beta_tm[:, g * 4 + h:g * 4 + h + 1]
            vdsl = otm[:, 3 * 4096 + g * 1024 + h * 256:3 * 4096 + g * 1024 + h * 256 + 256]
            csb2 = a_csp.bitcast(BF16)
            wts = [a_dt[:, 1056:1312], a_dt[:, 1568:1824], csb2[:, 0:256], csb2[:, 512:768]][h]
            ups = [a_dt[:, 1312:1568], a_dt[:, 1824:2080], csb2[:, 256:512], csb2[:, 768:1024]][h]
            wss = a_mx[:, h * 256:(h + 1) * 256]
            t_mb = S.add("vector", TS(mbs, Msl, bcol, None, ALU.mult),
                         waits=[X.get(("chaindone", g)), X.get(("scanfree", h)), t_beta], sem="dv", inc=1)
            t_wt = None
            for cc in range(2):
                t_wt = S.add("tensor", MM(pw[:, 256 + cc * 128:256 + cc * 128 + 128],
                                          kl2[:, cc * 128:(cc + 1) * 128], mbs, True, True),
                             waits=([t_mb] if cc == 0 else ()),
                             sem="mm", inc=(1 if cc else 0))
            t_wte = S.add("vector", COPY(wts, pw[:, 256:512]), waits=[t_wt], sem="dv", inc=1)
            yield
            t_ws = None
            for cc in range(2):
                t_ws = S.add("tensor", MM(pw[:, 0:256], wts[:, cc * 128:cc * 128 + 128],
                                          a_sbf[:, h * 512 + cc * 256:h * 512 + cc * 256 + 256],
                                          cc == 0, cc == 1),
                             waits=([t_wte, X.get(("sbf", h))] if cc == 0 else ()),
                             sem="mm", inc=(1 if cc else 0))
            t_wse = S.add("vector", COPY(wss, pw[:, 0:256]), waits=[t_ws], sem="dv", inc=1)
            yield
            t_u = S.add("tensor", MM(pw[:, 0:256], mbs, vdsl, True, True),
                        waits=[t_wse, X.get(("otm", 3, g, h, 0)), X.get(("otm", 3, g, h, 1))],
                        sem="mm", inc=1)
            t_up = S.add("vector", TT(ups, pw[:, 0:256], wss, ALU.subtract), waits=[t_u], sem="dv", inc=1)
            yield
            t_om = None
            for cc in range(2):
                qsl = a_q[:, (2 * h + cc) * TA + g * 128:(2 * h + cc) * TA + g * 128 + 128]
                t_om = S.add("tensor", MM(pw[:, 256:512], qsl,
                                          a_sbf[:, h * 512 + cc * 256:h * 512 + cc * 256 + 256],
                                          cc == 0, False),
                             waits=([t_up] if cc == 0 else ()), sem="mm", inc=0)
            t_om = S.add("tensor", MM(pw[:, 256:512], aTsl, ups, False, True), sem="mm", inc=1)
            dsl = otm[:, 2 * 4096 + g * 1024 + h * 256:2 * 4096 + g * 1024 + h * 256 + 256]
            t_oe = S.add("vector", TS(dsl, pw[:, 256:512],
                                      a_sc2[:, 4 + (g % 2) * 4 + h:5 + (g % 2) * 4 + h], None, ALU.mult),
                         waits=[t_om], sem="dv", inc=1)
            X[("otm", 2, g, h, 0)] = t_oe
            X[("otm", 2, g, h, 1)] = t_oe
            yield
            t_su = None
            for cc in range(2):
                t_su = S.add("tensor", MM(pw[:, cc * 256:(cc + 1) * 256],
                                          kl2[:, cc * 128:(cc + 1) * 128], ups, True, True),
                             waits=([t_oe] if cc == 0 else ()), sem="mm", inc=(1 if cc else 0))
            for cc in range(2):
                sfsl = a_sf[:, h * 512 + cc * 256:h * 512 + (cc + 1) * 256]
                if g == 0 and blk == 0:
                    S.add("vector", COPY(sfsl, pw[:, cc * 256:(cc + 1) * 256]), waits=[t_su])
                else:
                    S.add("vector", TT(sfsl, sfsl, pw[:, cc * 256:(cc + 1) * 256], ALU.add), waits=[t_su])
            t_se = S.add("vector", COPY(a_sbf[:, h * 512:(h + 1) * 512], a_sf[:, h * 512:(h + 1) * 512]),
                         sem="dv", inc=1)
            X[("sbf", h)] = t_se
            X[("scanfree", h)] = t_se
            X[("dt2", h)] = t_se
            if dbg is not None and blk == 0 and g == 0 and h == 0:
                S.add("gpsimd", DMA(dbg["wt"], a_dt[:, 1056:1312]), waits=[t_wte], sem="dbg", inc=16)
                S.add("gpsimd", DMA(dbg["up"], a_dt[:, 1312:1568]), waits=[t_up], sem="dbg", inc=16)
                S.add("gpsimd", DMA(dbg["sb0"], a_sbf[:, 0:512]), waits=[t_se], sem="dbg", inc=16)
                S.add("vector", NOP(), waits=[S.tok("dbg")], sem="dv", inc=1)

        for g in range(4):
            gens = ([gen_scan_h(g - 1, hh) for hh in range(4)] if g > 0 else [])
            rr(gens + [gen_prep(0, g), gen_prep(1, g)])
            rr([gen_prep(2, g), gen_prep(3, g)])
            if dbg is not None and blk == 0 and g == 0:
                wd_ = [scl_tok[hh] for hh in range(4)]
                S.add("gpsimd", DMA(dbg["ch0"], ch[:, 0:512]), waits=wd_ + [S.tok("dv")], sem="dbg", inc=16)
                S.add("gpsimd", DMA(dbg["ch1"], ch[:, 512:1024]), waits=wd_, sem="dbg", inc=16)
                S.add("gpsimd", DMA(dbg["ch8"], ch[:, 8 * 512:9 * 512]), waits=wd_, sem="dbg", inc=16)
                S.add("gpsimd", DMA(dbg["dt2"], a_dt2[:, 0:2048]), waits=wd_, sem="dbg", inc=16)
                S.add("gpsimd", DMA(dbg["sc2"], a_sc2[:, 0:8]), waits=wd_, sem="dbg", inc=16)
                S.add("vector", NOP(), waits=[S.tok("dbg")], sem="dv", inc=1)
            rr([gen_chain(g)])
            X["chainmm"] = S.tok("mm")
        rr([gen_scan_h(3, hh) for hh in range(4)])

        if aparts == "delta":
            continue
        t_feat = None
        for tc in range(4):
            fr = a_feat[:, tc * 64:(tc + 1) * 64]
            scr = a_sc.bitcast(BF16)[:, 0:1024]
            wpre = [X.get(("otm", kk, tc, hh, cc2)) for kk in range(4) for hh in range(4) for cc2 in range(2)]
            osl = lambda kk: otm[:, kk * 4096 + tc * 1024:kk * 4096 + tc * 1024 + 1024]
            o3 = lambda kk: osl(kk).rearrange("p (h d) -> p h d", d=256)
            f3 = lambda f: fr[:, f * 4:(f + 1) * 4].rearrange("p (h one) -> p h one", one=1)
            S.add("vector", RED(f3(0), o3(0), ALU.add), waits=wpre + [X.get("scact")])
            for i in range(1, 4):
                S.add("vector", RED(f3(2 * i), o3(i), ALU.add))
            for i in range(4):
                S.add("vector", TT(scr, osl(i), osl(i), ALU.mult))
                S.add("vector", RED(f3(2 * i + 1), scr.rearrange("p (h d) -> p h d", d=256), ALU.add))
            fidx = 8
            for i in range(4):
                for jj in range(i + 1, 4):
                    S.add("vector", TT(scr, osl(i), osl(jj), ALU.mult))
                    S.add("vector", RED(f3(fidx), scr.rearrange("p (h d) -> p h d", d=256), ALU.add))
                    fidx += 1
            t_feat = S.add("vector", (lambda e: e.nop()), sem="dv", inc=1)
        S.add("vector", TT(a_feat, a_feat, cn["fscale"], ALU.mult), waits=[t_feat])
        scr16 = a_sc[:, 0:64]
        for tcq in range(4):
            fv = a_feat[:, tcq * 64:(tcq + 1) * 64].rearrange("p (f h) -> p f h", h=4)
            sv16 = scr16[:, tcq * 16:(tcq + 1) * 16].rearrange("p (i h) -> p i h", h=4)
            S.add("vector", (lambda e, fv=fv, sv16=sv16: e.tensor_tensor(
                out=sv16, in0=fv[:, 0:8:2, :], in1=fv[:, 0:8:2, :], op=ALU.mult)))
            S.add("vector", (lambda e, fv=fv, sv16=sv16: e.scalar_tensor_tensor(
                out=fv[:, 1:8:2, :], in0=sv16, scalar=-256.0 / 255.0,
                in1=fv[:, 1:8:2, :], op0=ALU.mult, op1=ALU.add)))
        t_fb = S.add("vector", COPY(a_fT, a_feat), sem="dv", inc=1)
        t_ffm = None
        for tc in range(4):
            slot = tc % 4
            pt = p_c.bitcast(BF16)[0:64, slot * 128:slot * 128 + 128]
            t_t = S.add("tensor", TRANS(pt, a_fT[:, tc * 64:(tc + 1) * 64], cn["id_bf"]),
                        waits=[t_fb, X.get("ptrl"), X.get(("ptr", slot)), X.get("pc_free")], sem="mm", inc=1)
            t_ffm = S.add("vector", COPY(sb["ffm"][0:64, tc * 128:(tc + 1) * 128],
                                         pt[0:64, 0:128]), waits=[t_t], sem="dv", inc=1)
            X[("ptr", slot)] = t_ffm
            X["ptrl"] = t_ffm

        if aparts == "feats":
            continue
        t_rw = S.add("sync", DMA(arena[:, 0:9 * 2176].rearrange("p (k f) -> p k f", k=9),
                                 dwa["rw1"].rearrange("k p f -> p k f")),
                     waits=[X.get(("wfree", 0)), X.get(("wfree", 1))], sem="rw0", inc=16)
        t_l2 = None
        t_l2k = {}
        h1tok = {}
        for kf in range(17):
            par = kf % 2
            t_h1 = None
            for k in range(9):
                rhs = a_xn[:, k * TA:(k + 1) * TA] if k < 8 else sb["ffm"][0:64, 0:512]
                lhsT = arena[:, k * 2176 + kf * 128:k * 2176 + kf * 128 + 128]
                lh = lhsT if k < 8 else lhsT[0:64, :]
                t_h1 = S.add("tensor", MM(p_a[:, 0:512], lh, rhs, k == 0, k == 8),
                             waits=([t_rw, X.get("pa_free"), t_ffm, t_l2k.get(kf - 2)] if k == 0 else ()),
                             sem="mm", inc=(1 if k == 8 else 0))
            t_h1e = S.add("scalar", ACTD(a_h1[:, par * 512:(par + 1) * 512], p_a[:, 0:512], AF.Silu,
                                         bias=rb1[:, kf:kf + 1]),
                          waits=[t_h1], sem="act", inc=1)
            if dbg is not None and blk == 0 and kf < 2:
                S.add("gpsimd", DMA(dbg["h1_%d" % kf], a_h1[:, par * 512:(par + 1) * 512]),
                      waits=[t_h1e], sem="dbg", inc=16)
                S.add("scalar", NOP(), waits=[S.tok("dbg")], sem="act", inc=1)
            X["pa_free"] = t_h1e
            for tc in range(4):
                t_l2 = S.add("tensor", MM(p_d2[:, 448 + tc * 16:448 + tc * 16 + 16],
                                          a_h1[:, par * 512 + tc * 128:par * 512 + tc * 128 + 128],
                                          rw2[:, kf * 16:(kf + 1) * 16], True, True),
                             waits=([t_h1e, X.get("pd2o"), X.get("pd2_free"), X.get("l2ev")]
                                    if tc == 0 else [t_h1e]),
                             sem="mm", inc=(1 if tc == 3 else 0))
            if kf == 0:
                X["l2ev"] = S.add("vector", COPY(sb["l2acc"], p_d2[:, 448:512]),
                                  waits=[t_l2], sem="dv", inc=1)
            else:
                X["l2ev"] = S.add("vector", TT(sb["l2acc"], sb["l2acc"], p_d2[:, 448:512], ALU.add),
                                  waits=[t_l2], sem="dv", inc=1)
            t_l2k[kf] = X["l2ev"]
        X[("wfree", 0)] = S.tok("mm")
        X[("wfree", 1)] = S.tok("mm")
        if dbg is not None and blk == 0:
            S.add("gpsimd", DMA(dbg["l2a"], sb["l2acc"]), waits=[X.get("l2ev")], sem="dbg", inc=16)
            S.add("vector", NOP(), waits=[S.tok("dbg")], sem="dv", inc=1)
        l2 = a_sc[:, 0:64]
        t_l2e = S.add("vector", TT(l2, sb["l2acc"], rb2r, ALU.add),
                      waits=[X.get("l2ev"), X.get("scact")], sem="dv", inc=1)
        X["pd2o"] = t_l2e
        X["pd2_free"] = t_l2e
        l23 = l2.rearrange("p (s i) -> p s i", i=4)
        mx4 = a_sc[:, 64:80]
        mx43 = mx4.rearrange("p (s one) -> p s one", one=1)
        S.add("vector", RED(mx43, l23, ALU.max))
        t_sub = S.add("vector", TT(l23, l23, mx43.broadcast_to([128, 16, 4]), ALU.subtract), sem="dv", inc=1)
        t_exp = S.add("scalar", ACTF(l2, l2, AF.Exp), waits=[t_sub], sem="act", inc=1)
        S.add("vector", RED(mx43, l23, ALU.add), waits=[t_exp])
        S.add("vector", RECIP(a_sc[:, 80:96], mx4))
        S.add("vector", TT(l23, l23, a_sc[:, 80:96].rearrange("p (s one) -> p s one", one=1).broadcast_to([128, 16, 4]), ALU.mult))
        t_p = S.add("vector", TS(a_ptm, l2, 0.96, 0.01, ALU.mult, ALU.add), sem="dv", inc=1)
        X["scact"] = t_p

        if aparts == "rtr":
            continue
        t_on = {}
        t_onf = {}
        for tc in range(4):
            mxv = a_mx[:, 0:1024]
            m3 = mxv.rearrange("p (h d) -> p h d", d=256)
            p4 = a_ptm[:, tc * 16:(tc + 1) * 16].rearrange("p (h i) -> p h i", i=4)
            om = lambda kk: otm[:, kk * 4096 + tc * 1024:kk * 4096 + tc * 1024 + 1024]
            o3m = lambda kk: om(kk).rearrange("p (h d) -> p h d", d=256)
            S.add("vector", TT(m3, o3m(0), p4[:, :, 0:1].broadcast_to([128, 4, 256]), ALU.mult),
                  waits=[t_p, X.get("mixact")])
            for i in range(1, 4):
                scr3 = a_sc[:, 0:1024].rearrange("p (h d) -> p h d", d=256)
                S.add("vector", TT(scr3, o3m(i), p4[:, :, i:i + 1].broadcast_to([128, 4, 256]), ALU.mult))
                S.add("vector", TT(mxv, mxv, a_sc[:, 0:1024], ALU.add))
            for rnd in range(2):
                sq = a_sc[:, 0:1024]
                S.add("vector", TT(sq, mxv, mxv, ALU.mult))
                t_red = S.add("vector", RED(a_mx[:, 1024:1028].rearrange("p (h one) -> p h one", one=1),
                                            sq.rearrange("p (h d) -> p h d", d=256), ALU.add), sem="dv", inc=1)
                t_rms = S.add("scalar", ACTF(a_mx[:, 1028:1032], a_mx[:, 1024:1028], AF.Sqrt,
                                             bias=cn["eps"][0:128, 0:1], scale=1.0 / 256.0),
                              waits=[t_red], sem="act", inc=1)
                S.add("vector", RECIP(a_mx[:, 1024:1028], a_mx[:, 1028:1032]), waits=[t_rms])
                S.add("vector", TT(m3, m3, a_mx[:, 1024:1028].rearrange("p (h one) -> p h one", one=1).broadcast_to([128, 4, 256]), ALU.mult))
                if rnd == 0:
                    S.add("vector", TT(mxv, mxv, mixw, ALU.mult))
            t_onx = S.add("vector", TT(a_on[:, (tc % 2) * 1024:(tc % 2) * 1024 + 1024], mxv, onormw, ALU.mult),
                          waits=[X.get(("onsl", tc % 2))], sem="dv", inc=1)
            t_on[tc] = t_onx
            X["mixact"] = t_onx
            for c8 in range(8):
                slot = c8 % 4
                pt = p_c.bitcast(BF16)[:, slot * 128:slot * 128 + 128]
                t_t = S.add("tensor", TRANS(pt, a_on[:, (tc % 2) * 1024 + c8 * 128:(tc % 2) * 1024 + c8 * 128 + 128],
                                            cn["id_bf"]),
                            waits=[t_onx, X.get("ptrl"), X.get(("ptr", slot))], sem="mm", inc=1)
                t_e = S.add("vector", COPY(a_onf[:, c8 * TA + tc * 128:c8 * TA + tc * 128 + 128], pt),
                            waits=[t_t], sem="dv", inc=1)
                X[("ptr", slot)] = t_e
                X["ptrl"] = t_e
                t_onf[(tc, c8)] = t_e
            X[("onsl", tc % 2)] = S.tok("mm")
        X["scact"] = t_on[3]
        X["pc_free"] = S.tok("dv")
        ld_w(3)
        for c in range(8):
            t_wo = None
            for k in range(8):
                wsl = arena[:, 8192 + k * 1024 + c * 128:8192 + k * 1024 + c * 128 + 128]
                t_wo = S.add("tensor", MM(p_b2[:, 0:512], wsl, a_onf[:, k * TA:(k + 1) * TA],
                                          k == 0, k == 7),
                             waits=([wtok[3], X.get("pb2_free")] + [t_onf[(tc2, k)] for tc2 in range(4)]
                                    if k == 0 else [t_onf[(tc2, k)] for tc2 in range(4)]),
                             sem="mm", inc=(1 if k == 7 else 0))
            par = c % 2
            t_se2 = S.add("vector", COPY(a_sout[:, par * 512:(par + 1) * 512], p_b2[:, 0:512]),
                          waits=[t_wo, X.get(("sof", par))], sem="dv", inc=1)
            X["pb2_free"] = t_se2
            t_sd = S.add("scalar", DMA(d_s[c, :, t0:t0 + TA], a_sout[:, par * 512:(par + 1) * 512]),
                         waits=[t_se2], sem="so%d" % par, inc=16)
            X[("sof", par)] = t_sd
        X[("wfree", 1)] = S.tok("mm")
    if dbg is not None and aparts == "all":
        w = [X[("sof", 0)], X[("sof", 1)]]
        for nm, tile in [("xn", a_xn), ("q", a_q), ("k", a_k), ("v", a_v),
                         ("beta", beta_tm), ("otm", otm), ("feat", a_feat),
                         ("ptm", a_ptm), ("onf", a_onf), ("sbf", a_sbf), ("ffm", sb["ffm"])]:
            S.add("gpsimd", DMA(dbg[nm], tile), waits=w, sem="dbg", inc=16)
    return


def emit_convert(S, d_in, dws, ne):
    for e in range(ne):
        for m, nm in [(0, "w_gate"), (1, "w_up")]:
            src = d_in[nm][e]
            for g in range(8):
                for k in range(8):
                    sap = src.rearrange("(kk p) (gg i f) -> kk gg p i f", p=128, i=4, f=128)[k, g]
                    S.add("gpsimd", DMA(dws["wgu"][e, m, g, :, k], sap), sem="cvt", inc=16)
        src = d_in["w_down"][e]
        for j in range(8):
            sap = src.rearrange("(i p) (jj c) -> jj p i c", p=128, c=128)[j]
            S.add("gpsimd", DMA(dws["wd"][e, j], sap), sem="cvt", inc=16)


def build_nc(cfg):
    nblk, ne, use_s = cfg["nblk"], cfg["ne"], cfg["use_s"]
    use_attn = cfg.get("use_attn", False)
    nc = bass.Bass("TRN2", target_bir_lowering=False)

    d_in = {}
    d_in["x"] = nc.dram_tensor("x", [D, L], F32, kind="ExternalInput").ap()
    for nm, shp in [("w_gate", [NE, D, DFF]), ("w_up", [NE, D, DFF]), ("w_down", [NE, DFF, D]),
                    ("router_wT", [D, NE]), ("wqkvo", [4, D, D]), ("rw1p", [1152, 2176]),
                    ("rw2p", [2176, 16]), ("wbr", [8, 128, 4]), ("cvwr", [128, 8 * 44]),
                    ("rb1r", [128, 17]), ("rb2r", [128, 64]),
                    ("mixwr", [128, 1024]), ("onormr", [128, 1024])]:
        d_in[nm] = nc.dram_tensor(nm, shp, F32, kind="ExternalInput").ap()
    d_c = {}
    for nm, shp, dt in [("ones_bf", [128, 1], BF16), ("ones_row", [1, 128], BF16),
                        ("eps", [128, 2], F32), ("iota8", [128, 64], F32),
                        ("iota8r", [128, 64], F32), ("E8", [8, NE * 128], BF16),
                        ("id_bf", [128, 128], BF16), ("mkA", [128, 128], BF16),
                        ("mkAu", [128, 128], BF16), ("mkT", [128, 128], BF16),
                        ("mD16u", [128, 128], BF16), ("mD16l", [128, 128], BF16),
                        ("mB16u", [128, 128], BF16), ("mB16l", [128, 128], BF16),
                        ("mB32u", [128, 128], BF16), ("mB32l", [128, 128], BF16),
                        ("mB64u", [128, 128], BF16), ("mB64l", [128, 128], BF16),
                        ("fscale", [128, 256], F32)]:
        d_c[nm] = nc.dram_tensor("c_" + nm, shp, dt, kind="ExternalInput").ap()

    d_out = nc.dram_tensor("out", [8, 128, L], F32, kind="ExternalOutput").ap()
    d_s = nc.dram_tensor("s_scr", [8, 128, L], F32,
                         kind=("ExternalOutput" if cfg.get("dbg_s") else "Internal")).ap()

    dws = {
        "wgu": nc.dram_tensor("wgu_s", [NE, 2, 8, 128, 8, 4, 128], BF16, kind="Internal").ap(),
        "wd": nc.dram_tensor("wd_s", [NE, 8, 128, 32, 128], BF16, kind="Internal").ap(),
        "cwscr": nc.dram_tensor("cw_scr", [64, 128], BF16, kind="Internal").ap(),
    }
    dws2_outer = {
        "wqkv": nc.dram_tensor("wqkv_s", [4, 128, 8, 1024], BF16, kind="Internal").ap(),
        "rw1": nc.dram_tensor("rw1_s", [9, 128, 2176], BF16, kind="Internal").ap(),
    }

    cnt = {}
    with contextlib.ExitStack() as es:
        def sbt(name, shape, dt):
            return es.enter_context(nc.sbuf_tensor(name, shape, dt)).ap()
        sems = {k: es.enter_context(nc.semaphore(k))
                for k in ["ld", "cvt", "mm", "dv", "act", "fin", "xa", "xv", "sv", "cwd",
                          "wg0", "wg1", "wu0", "wu1", "wd0", "wd1", "st0", "st1",
                          "wq0", "wq1", "rw0", "so0", "so1", "gp", "dbg"]}
        cn = {"ones_bf": sbt("k_ones_bf", [128, 1], BF16),
              "eps": sbt("k_eps", [128, 2], F32),
              "ones_row": sbt("k_ones_row", [1, 128], BF16),
              "iota8": sbt("k_iota8", [128, 64], F32),
              "iota8r": sbt("k_iota8r", [128, 64], F32),
              "id_bf": sbt("k_id_bf", [128, 128], BF16),
              "mkA": sbt("k_mkA", [128, 128], BF16),
              "mkAu": sbt("k_mkAu", [128, 128], BF16),
              "mkT": sbt("k_mkT", [128, 128], BF16),
              "mD16u": sbt("k_mD16u", [128, 128], BF16),
              "mD16l": sbt("k_mD16l", [128, 128], BF16),
              "mB16u": sbt("k_mB16u", [128, 128], BF16),
              "mB16l": sbt("k_mB16l", [128, 128], BF16),
              "mB32u": sbt("k_mB32u", [128, 128], BF16),
              "mB32l": sbt("k_mB32l", [128, 128], BF16),
              "mB64u": sbt("k_mB64u", [128, 128], BF16),
              "mB64l": sbt("k_mB64l", [128, 128], BF16),
              "fscale": sbt("k_fscale", [128, 256], F32),
              "E8": sbt("k_E8", [8, NE * 128], BF16)}
        s_rw = sbt("s_rw", [128, 8 * NE], BF16)
        s_rw2 = sbt("s_rw2", [128, 17 * 16], BF16)
        s_wbr = sbt("s_wbr", [128, 8 * 4], BF16)
        s_cvw = sbt("s_cvw", [128, 8 * 44], F32)
        s_rb1 = sbt("s_rb1", [128, 17], F32)
        s_rb2 = sbt("s_rb2", [128, 64], F32)
        s_mixw = sbt("s_mixw", [128, 1024], BF16)
        s_onorm = sbt("s_onorm", [128, 1024], BF16)
        dws2 = dws2_outer

        S0 = Sched(cnt)
        for nm in cn:
            S0.add("sync", DMA(cn[nm], d_c[nm]), sem="ld", inc=16)
        S0.add("gpsimd", DMA(s_rw.rearrange("p (k e) -> p k e", k=8),
                             d_in["router_wT"].rearrange("(k p) e -> p k e", p=128)), sem="cvt", inc=16)
        emit_convert(S0, d_in, dws, ne)
        if use_attn:
            for m in range(4):
                src = d_in["wqkvo"][m]
                for k in range(8):
                    S0.add("gpsimd", DMA(dws2["wqkv"][m, :, k],
                                         src.rearrange("(k p) f -> k p f", p=128)[k]),
                           sem="cvt", inc=16)
            for k in range(9):
                S0.add("gpsimd", DMA(dws2["rw1"][k],
                                     d_in["rw1p"].rearrange("(k p) f -> k p f", p=128)[k]),
                       sem="cvt", inc=16)
            S0.add("gpsimd", DMA(s_rw2.rearrange("p (k e) -> p k e", e=16),
                                 d_in["rw2p"].rearrange("(k p) e -> p k e", p=128)), sem="cvt", inc=16)
            S0.add("gpsimd", DMA(s_wbr.rearrange("p (k e) -> p k e", e=4),
                                 d_in["wbr"].rearrange("k p e -> p k e")), sem="cvt", inc=16)
            S0.add("gpsimd", DMA(s_mixw, d_in["mixwr"]), sem="cvt", inc=16)
            S0.add("gpsimd", DMA(s_onorm, d_in["onormr"]), sem="cvt", inc=16)
            S0.add("sync", DMA(s_cvw, d_in["cvwr"]), sem="ld", inc=16)
            S0.add("sync", DMA(s_rb1, d_in["rb1r"]), sem="ld", inc=16)
            S0.add("sync", DMA(s_rb2, d_in["rb2r"]), sem="ld", inc=16)
        S0.add("gpsimd", NOP(), waits=[("cvt", cnt.get("cvt", 0))], sem="fin", inc=1)
        S0.add("sync", NOP(), waits=[("ld", cnt.get("ld", 0))], sem="fin", inc=1)
        with nc.Block() as blk0:
            S0.emit(blk0, sems)

        if use_attn:
            with contextlib.ExitStack() as esa:
                def sbta(name, shape, dt):
                    return esa.enter_context(nc.sbuf_tensor(name, shape, dt)).ap()
                sba = {
                    "x": sbta("a_x", [128, 8 * TA], BF16),
                    "xn": sbta("a_xn", [128, 8 * TA], BF16),
                    "z": sbta("a_z", [128, 2 * 515], F32),
                    "cs": sbta("a_cs", [128, 512], F32),
                    "csp": sbta("a_csp", [128, 512], F32),
                    "q": sbta("a_q", [128, 8 * TA], BF16),
                    "k": sbta("a_k", [128, 8 * TA], BF16),
                    "v": sbta("a_v", [128, 8 * 640], BF16),
                    "lo": sbta("a_lo", [128, 2 * 512], BF16),
                    "mi": sbta("a_mi", [128, 2 * 512], BF16),
                    "otm": sbta("a_otm", [128, 4 * 4096], BF16),
                    "feat": sbta("a_feat", [128, 256], F32),
                    "fT": sbta("a_fT", [128, 256], BF16),
                    "ffm": sbta("a_ffm", [64, 512], BF16),
                    "ptm": sbta("a_ptm", [128, 64], F32),
                    "h1": sbta("a_h1", [128, 2 * 512], BF16),
                    "mx": sbta("a_mx", [128, 1040], F32),
                    "on": sbta("a_on", [128, 2 * 1024], BF16),
                    "onf": sbta("a_onf", [128, 8 * TA], BF16),
                    "stat": sbta("a_stat", [1, 2 * TA], F32),
                    "statr": sbta("a_statr", [1, TA], BF16),
                    "tl": sbta("a_tl", [128, 72 + 192], F32),
                    "ch": sbta("a_ch", [128, 12 * 512], BF16),
                    "sc": sbta("a_sc", [128, 1024], F32),
                    "sbf": sbta("a_sbf", [128, 4 * 512], BF16),
                    "dt": sbta("a_dt", [128, 2080], BF16),
                    "dt2": sbta("a_dt2", [128, 2 * 1024], BF16),
                    "mb": sbta("a_mb", [128, 512], BF16),
                    "xm": sbta("a_xm", [128, 768], BF16),
                    "row": sbta("a_row", [1, 768], BF16),
                    "sc2": sbta("a_sc2", [128, 12], F32),
                    "l2acc": sbta("a_l2acc", [128, 64], F32),
                    "beta": sbta("a_beta", [128, 16], F32),
                    "arena": sbta("a_arena", [128, 9 * 2176], BF16),
                    "rw2": s_rw2, "wb": s_wbr, "cvw": s_cvw,
                    "rb1": s_rb1, "rb2": s_rb2, "mixw": s_mixw, "onorm": s_onorm,
                }
                psa = {k: esa.enter_context(nc.psum_tensor("pa_" + k, [128, 512], F32)).ap()
                       for k in ["a", "b2", "c", "d2", "e", "f", "g2", "h2"]}
                sba["sf32"] = sbta("a_sf32", [128, 2048], F32)
                SA = Sched(cnt)
                SA.add("tensor", NOP(), waits=[("fin", 2)])
                SA.add("sync", NOP(), waits=[("fin", 2)])
                SA.add("vector", NOP(), waits=[("fin", 2)], sem="dv", inc=1)
                SA.add("gpsimd", NOP(), waits=[("fin", 2)])
                dbg = None
                if cfg.get("dbg_s"):
                    dbg = {}
                    for nm, shp, dt in [("vd0", [128, 4096], BF16),
                                    ("row", [1, 384], BF16), ("xm", [128, 384], BF16),
                                    ("l2a", [128, 64], F32),
                                    ("h1_0", [128, 512], BF16), ("h1_1", [128, 512], BF16),
                                    ("wt", [128, 256], BF16), ("up", [128, 256], BF16),
                                    ("sb0", [128, 512], BF16),
                                    ("ch0", [128, 512], BF16), ("ch1", [128, 512], BF16),
                                    ("ch8", [128, 512], BF16), ("dt2", [128, 2048], BF16),
                                    ("sc2", [128, 8], F32), ("M", [128, 512], BF16),
                                    ("xn", [128, 4096], BF16), ("q", [128, 4096], BF16),
                                        ("k", [128, 4096], BF16), ("v", [128, 8 * 640], BF16),
                                        ("beta", [128, 16], F32), ("otm", [128, 16384], BF16),
                                        ("feat", [128, 256], F32), ("ptm", [128, 64], F32),
                                        ("onf", [128, 4096], BF16), ("sbf", [128, 2048], BF16),
                                        ("ffm", [64, 512], BF16)]:
                        dbg[nm] = nc.dram_tensor("dbg_" + nm, shp, dt, kind="ExternalOutput").ap()
                emit_attn(SA, d_in["x"], d_s, dws2_outer, sba, psa, cn, nblka=cfg.get("nblka", NBLKA),
                          dbg=dbg,
                          aparts=cfg.get("aparts", "all"))
                SA.add("gpsimd", NOP(), waits=[("so0", cnt.get("so0", 0)), ("so1", cnt.get("so1", 0)),
                                               ("dbg", cnt.get("dbg", 0))],
                       sem="fin", inc=1)
                with nc.Block() as blka:
                    SA.emit(blka, sems)

        with contextlib.ExitStack() as es1:
            def sbt1(name, shape, dt):
                return es1.enter_context(nc.sbuf_tensor(name, shape, dt)).ap()
            sb = {"xn": sbt1("b_xn", [128, 8 * TB], BF16),
                  "xnup": sbt1("b_xnup", [128, 8 * TB], BF16),
                  "h": sbt1("b_h", [128, 32 * TB], BF16),
                  "acc": sbt1("b_acc", [128, 8 * TB], F32),
                  "wg": sbt1("b_wg", [128, 2 * 8 * 4 * 128], BF16),
                  "wu": sbt1("b_wu", [128, 2 * 8 * 4 * 128], BF16),
                  "wd": sbt1("b_wd", [128, 2 * 32 * 128], BF16),
                  "sg": sbt1("b_sg", [128, 1024], F32),
                  "rw": s_rw,
                  "stat": sbt1("b_stat", [97, TB], F32),
                  "statr": sbt1("b_statr", [1, TB], BF16),
                  "ltm": sbt1("b_ltm", [128, 64], F32),
                  "top": sbt1("b_top", [128, 256], F32),
                  "cw": sbt1("b_cw", [8, TB], BF16),
                  "msk1": sbt1("b_msk1", [128, 64], F32),
                  "msk2": sbt1("b_msk2", [128, 64], F32),
                  "mskd": sbt1("b_mskd", [128, 64], F32),
                  "t1m": sbt1("b_t1m", [128, 64], F32),
                  "cwtm": sbt1("b_cwtm", [128, 64], F32),
                  "cwr": sbt1("b_cwr", [128, 64], BF16),
                  "cwT": sbt1("b_cwT", [64, 128], BF16)}
            psb = {k: es1.enter_context(nc.psum_tensor("p_" + k, [128, 1024], F32)).ap()
                   for k in ["g", "u", "d", "b"]}

            S1 = Sched(cnt)
            if cfg.get("skip_moe"):
                S1.add("gpsimd", NOP(), waits=[("fin", 3)], sem="fin", inc=1)
                with nc.Block() as blk1:
                    S1.emit(blk1, sems)
                return nc
            fin_need = 3 if use_attn else 2
            S1.add("tensor", NOP(), waits=[("fin", fin_need)])
            S1.add("sync", NOP(), waits=[("fin", fin_need)])
            emit_moe(S1, d_in["x"], d_out, d_s if use_s else None, dws, sb, psb, cn, use_s, nblk, ne, parts=cfg.get("parts", "all"))
            S1.add("gpsimd", NOP(), waits=[("st0", cnt.get("st0", 0)), ("st1", cnt.get("st1", 0))],
                   sem="fin", inc=1)
            with nc.Block() as blk1:
                S1.emit(blk1, sems)
    return nc


def _msk(f):
    i, j = np.mgrid[0:128, 0:128]
    return f(i, j).astype(np.float32).astype(BF)


def _consts_host():
    ONES = np.ones((128, 128), np.float32)
    fs = np.zeros(64, np.float32)
    for f in range(14):
        v = 1.0 / 255.0 if (f < 8 and f % 2 == 1) else 1.0 / 256.0
        fs[f * 4:(f + 1) * 4] = v
    return {
        "c_mkA": (-np.tril(ONES, -1)).astype(BF),
        "c_mkAu": (-np.triu(ONES, 1)).astype(BF),
        "c_mkT": np.triu(ONES, 0).astype(BF),
        "c_mD16u": _msk(lambda i, j: (j > i) & (i // 16 == j // 16)),
        "c_mD16l": _msk(lambda i, j: (i > j) & (i // 16 == j // 16)),
        "c_mB16u": _msk(lambda i, j: (j // 16 == i // 16 + 1) & (i // 32 == j // 32)),
        "c_mB16l": _msk(lambda i, j: (i // 16 == j // 16 + 1) & (i // 32 == j // 32)),
        "c_mB32u": _msk(lambda i, j: (j // 32 == i // 32 + 1) & (i // 64 == j // 64)),
        "c_mB32l": _msk(lambda i, j: (i // 32 == j // 32 + 1) & (i // 64 == j // 64)),
        "c_mB64u": _msk(lambda i, j: (j // 64 == i // 64 + 1)),
        "c_mB64l": _msk(lambda i, j: (i // 64 == j // 64 + 1)),
        "c_fscale": np.tile(fs, (128, 4)),
        "c_ones_bf": np.ones((128, 1), np.float32).astype(BF),
        "c_iota8": np.tile(np.arange(8, dtype=np.float32), (128, 8)),
        "c_iota8r": np.tile(np.arange(7, -1, -1, dtype=np.float32), (128, 8)),
        "c_eps": np.tile(np.array([1e-5, 1e-12], np.float32), (128, 1)),
        "c_id_bf": np.eye(128, dtype=np.float32).astype(BF),
        "c_ones_row": np.ones((1, 128), np.float32).astype(BF),

        "c_E8": np.repeat(np.eye(8, dtype=np.float32), 128, axis=1).astype(BF),
    }


def make_in_maps(inputs):
    x = np.asarray(inputs["L0_attn_hidden_states"])
    base = {
        "w_gate": np.ascontiguousarray(np.asarray(inputs["L0_ffn_w_gate"], np.float32)),
        "w_up": np.ascontiguousarray(np.asarray(inputs["L0_ffn_w_up"], np.float32)),
        "w_down": np.ascontiguousarray(np.asarray(inputs["L0_ffn_w_down"], np.float32)),
        "router_wT": np.ascontiguousarray(np.asarray(inputs["L0_ffn_router_w"], np.float32).T),
    }
    g32 = lambda k, shp: np.asarray(inputs[k], np.float32) if k in inputs else np.zeros(shp, np.float32)
    wq = g32("L0_attn_Wq", (D, D)); wk = g32("L0_attn_Wk", (D, D))
    wv = g32("L0_attn_Wv", (D, D)); wo = g32("L0_attn_Wo", (D, D))
    base["wqkvo"] = np.ascontiguousarray(np.stack([wq, wk, wv, wo], 0))
    rw1 = g32("L0_attn_r_w1", (1080, 2160))
    rw1p = np.zeros((1152, 2176), np.float32); rw1p[:1080, :2160] = rw1
    base["rw1p"] = rw1p
    rw2 = g32("L0_attn_r_w2", (2160, 16))
    rw2p = np.zeros((2176, 16), np.float32); rw2p[:2160] = rw2
    base["rw2p"] = rw2p
    base["wbr"] = np.ascontiguousarray(g32("L0_attn_Wb", (D, H)).reshape(8, 128, 4))
    cv = np.zeros((128, 8 * 44), np.float32)
    cq = g32("L0_attn_conv_q_w", (D, 4)).reshape(8, 128, 4)
    ck = g32("L0_attn_conv_k_w", (D, 4)).reshape(8, 128, 4)
    cvv = g32("L0_attn_conv_v_w", (D, 4)).reshape(8, 128, 4)
    lw = g32("L0_attn_local_w", (D, 7)).reshape(8, 128, 7)
    mw = g32("L0_attn_mid_w", (D, 25)).reshape(8, 128, 25)
    for c in range(8):
        cv[:, c * 44 + 0:c * 44 + 4] = cq[c]
        cv[:, c * 44 + 4:c * 44 + 8] = ck[c]
        cv[:, c * 44 + 8:c * 44 + 12] = cvv[c]
        cv[:, c * 44 + 12:c * 44 + 19] = lw[c]
        cv[:, c * 44 + 19:c * 44 + 44] = mw[c]
    base["cvwr"] = cv
    rb1 = g32("L0_attn_r_b1", (2160,))
    rb1p = np.zeros(2176, np.float32); rb1p[:2160] = rb1
    base["rb1r"] = np.ascontiguousarray(rb1p.reshape(17, 128).T)
    base["rb2r"] = np.tile(g32("L0_attn_r_b2", (16,)), (128, 4))
    base["mixwr"] = np.tile(g32("L0_attn_mix_w", (H, DH)).reshape(-1), (128, 1))
    base["onormr"] = np.tile(np.tile(g32("L0_attn_onorm_w", (DH,)), H), (128, 1))
    base.update(_consts_host())
    in_maps = []
    for c in range(NCORES):
        m = dict(base)
        m["x"] = np.ascontiguousarray(np.asarray(x[c], np.float32).T)
        in_maps.append(m)
    return in_maps


def kernel(**inputs):
    in_maps = make_in_maps(inputs)
    nc = build_nc(CFG)
    res = run_bass_kernel_spmd(nc, in_maps, core_ids=list(range(NCORES)))
    outs = []
    for c in range(NCORES):
        o = np.asarray(res.results[c]["out"])
        outs.append(o.reshape(D, L).T)
    return np.stack(outs, 0)


# version 06
# speedup vs baseline: 1.0533x; 1.0533x over previous
import contextlib
import numpy as np
import ml_dtypes
import concourse.bass as bass
import concourse.mybir as mybir
from concourse.bass_utils import run_bass_kernel_spmd

F32 = mybir.dt.float32
F32R = mybir.dt.float32r
BF16 = mybir.dt.bfloat16
AF = mybir.ActivationFunctionType
ALU = mybir.AluOpType
AX = mybir.AxisListType
BF = ml_dtypes.bfloat16

ENGINES = ["sync", "gpsimd", "tensor", "vector", "scalar"]
ENG_MAP = {"pool": "gpsimd"}

B, L, D = 8, 8192, 1024
H, DH = 4, 256
NE, DFF = 8, 4096
TB = 1024
NCORES = 8

CFG = {"nblk": L // TB, "ne": NE, "use_s": True, "use_attn": True}


AUTO_CHAIN = {"vector": "dv", "scalar": "act", "pool": "gp"}


class Sched:

    def __init__(self, cnt=None):
        self.ops = []
        self.cnt = cnt if cnt is not None else {}

    def add(self, engine, fn, waits=(), sem=None, inc=0):
        waits = [w for w in waits if w is not None]
        if engine in AUTO_CHAIN:
            csem = AUTO_CHAIN[engine]
            waits.append((csem, self.cnt.get(csem, 0)))
            if inc == 0:
                sem, inc = csem, 1
        tok = None
        if inc:
            assert sem is not None
            self.cnt[sem] = self.cnt.get(sem, 0) + inc
            tok = (sem, self.cnt[sem])
        self.ops.append((engine, fn, waits, sem, inc))
        return tok

    def tok(self, sem):
        return (sem, self.cnt.get(sem, 0))

    def emit(self, block, sems):
        for engname in ENGINES:
            ops = [o for o in self.ops if ENG_MAP.get(o[0], o[0]) == engname]
            if not ops:
                continue

            def body(eng, ops=ops):
                last_waited = {}
                for _, fn, waits, sem, inc in ops:
                    for (s, v) in waits:
                        if v <= 0 or last_waited.get(s, -1) >= v:
                            continue
                        eng.wait_ge(sems[s], v)
                        last_waited[s] = v
                    inst = fn(eng)
                    if inc:
                        inst.then_inc(sems[sem], inc)
            getattr(block, engname)(body)


def MM(out, lhsT, rhs, start, stop):
    return lambda e: e.matmul(out, lhsT, rhs, start=start, stop=stop)

def TRANS(out, in_, ident):
    def f(e):
        e.transpose(out, in_, ident)
        return e.drain()
    return f


def MMD(out, lhsT, rhs, start, stop):
    def f(e):
        e.matmul(out, lhsT, rhs, start=start, stop=stop)
        return e.drain()
    return f

def DMA(out, in_):
    return lambda e: e.dma_start(out=out, in_=in_)

def TT(out, a, b, op):
    return lambda e: e.tensor_tensor(out=out, in0=a, in1=b, op=op)

def TS(out, in0, s1, s2, op0, op1=None):
    if op1 is None:
        return lambda e: e.tensor_scalar(out=out, in0=in0, scalar1=s1, scalar2=None, op0=op0)
    return lambda e: e.tensor_scalar(out=out, in0=in0, scalar1=s1, scalar2=s2, op0=op0, op1=op1)

def STT(out, in0, s, in1, op0, op1):
    return lambda e: e.scalar_tensor_tensor(out=out, in0=in0, scalar=s, in1=in1, op0=op0, op1=op1)

def ACTF(out, in_, func, bias=0.0, scale=1.0):
    return lambda e: e.activation(out, in_, func, bias=bias, scale=scale)


def ACTD(out, in_, func, bias=0.0, scale=1.0):
    def f(e):
        e.activation(out, in_, func, bias=bias, scale=scale)
        return e.drain()
    return f

def COPY(out, in_):
    return lambda e: e.tensor_copy(out, in_)

def RED(out, in_, op, axis=AX.X):
    return lambda e: e.tensor_reduce(out=out, in_=in_, axis=axis, op=op)

def RECIP(out, in_):
    return lambda e: e.reciprocal(out, in_)

def NOP():
    return lambda e: e.nop()


class MoeCtx:
    def __init__(self):
        self.gate_free = {}
        self.up_free = {}
        self.down_free = {}
        self.pb_free = None
        self.gu_free = {}
        self.wd_free = {}
        self.acc_free = {}
        self.out_done = {}
        self.sg_free = {0: None, 1: None}
        self.all_mm_done_dv = None
        self.prev_act = None
        self.comb_last = None
        self.cwt_free = None
        self.cw_free = None


def emit_moe(S, dx, d_out, d_s, dws, sb, psb, cn, use_s, nblk, ne, parts="all"):
    C = MoeCtx()
    s_xn, s_xnup, s_h, s_acc = sb["xn"], sb["xnup"], sb["h"], sb["acc"]
    s_wg, s_wu, s_wd, s_sg = sb["wg"], sb["wu"], sb["wd"], sb["sg"]
    s_rw, s_stat, s_statr = sb["rw"], sb["stat"], sb["statr"]
    s_ltm, s_top, s_cw = sb["ltm"], sb["top"], sb["cw"]
    s_xbf = s_xnup[:, 0:2 * TB]
    s_xsq = s_xnup[:, 2 * TB:4 * TB]
    s_outb = s_xnup[:, 4 * TB:6 * TB].bitcast(F32)
    p_g, p_u, p_d, p_b = psb["g"], psb["u"], psb["d"], psb["b"]

    x_blk = s_h.bitcast(F32)[:, 0:8 * TB]
    xv = s_h.bitcast(F32)[:, 8 * TB:12 * TB]
    sv = s_h.bitcast(F32)[:, 12 * TB:16 * TB]

    for blk in range(nblk):
        t0 = blk * TB
        t_x = S.add("gpsimd", DMA(x_blk.rearrange("p (k t) -> p k t", k=8),
                                  dx.rearrange("(k p) t -> p k t", p=128)[:, :, t0:t0 + TB]),
                    waits=[C.all_mm_done_dv], sem="xa", inc=16)
        tok_k = {}
        tok_ka = {}
        t_sum_done = None
        for k in range(8):
            par = k % 2
            xb = s_xbf[:, par * TB:(par + 1) * TB]
            xq = s_xsq[:, par * TB:(par + 1) * TB]
            xk = x_blk[:, k * TB:(k + 1) * TB]
            t_bf = S.add("vector", COPY(xb, xk),
                         waits=[t_x, tok_k.get(k - 2), tok_ka.get(k - 2)], sem="dv", inc=1)
            tok_ka[k] = S.add("scalar", ACTF(xq, xb, AF.Square),
                              waits=[t_bf, tok_k.get(k - 2)], sem="act", inc=1)
            for s in range(2):
                sl = slice(s * 512, (s + 1) * 512)
                S.add("tensor", MM(p_d[0:1, s * 512:(s + 1) * 512], cn["ones_bf"], xb[:, sl], k == 0, k == 7),
                      waits=[t_bf])
                tok_k[k] = S.add("tensor", MM(p_d[32:33, s * 512:(s + 1) * 512], cn["ones_bf"], xq[:, sl], k == 0, k == 7),
                                 waits=[tok_ka[k]], sem="mm", inc=1)
        t_stats_mm = tok_k[7]
        st0 = s_stat[0:1, :]
        rrow = s_statr[0:1, :]
        t_mu = S.add("vector", TS(st0, p_d[0:1, 0:TB], 1.0 / D, None, ALU.mult),
                     waits=[t_stats_mm], sem="dv", inc=1)
        t_mur = S.add("vector", COPY(rrow, st0), waits=[t_mu], sem="dv", inc=1)
        t_xn_last = None
        xn_scr = s_sg[:, 0:512]
        t_bm = {}
        for s in range(2):
            sl = slice(s * 512, (s + 1) * 512)
            t_bm[s] = S.add("tensor", MM(p_b[:, sl], cn["ones_row"], rrow[:, sl], True, True),
                            waits=[t_mur, C.pb_free], sem="mm", inc=1)
        st1 = s_xnup[:, 6 * TB:8 * TB].bitcast(F32)[0:1, 0:TB]
        S.add("vector", TT(st1, st0, st0, ALU.mult), waits=[t_mur])
        S.add("vector", STT(st0, p_d[32:33, 0:TB], 1.0 / D, st1, ALU.mult, ALU.subtract),
              waits=[t_stats_mm])
        S.add("vector", TS(st1, st0, 1e-5, None, ALU.add))
        t_inv = S.add("vector", RECIP(st0, st1), sem="dv", inc=1)
        t_rsr = S.add("scalar", ACTD(rrow, st0, AF.Sqrt), waits=[t_inv, t_bm[0], t_bm[1]],
                      sem="act", inc=1)
        for s in range(2):
            sl = slice(s * 512, (s + 1) * 512)
            t_br = S.add("tensor", MM(p_u[:, sl], cn["ones_row"], rrow[:, sl], True, True),
                         waits=[t_rsr, C.up_free.get(s)], sem="mm", inc=1)
            for k in range(8):
                xk = x_blk[:, k * TB + s * 512: k * TB + (s + 1) * 512]
                S.add("vector", TT(xn_scr, xk, p_b[:, sl], ALU.subtract),
                      waits=[t_bm[s], C.sg_free[0]])
                t_xn_last = S.add("vector", TT(s_xn[:, k * TB + s * 512: k * TB + (s + 1) * 512],
                                               xn_scr, p_u[:, sl], ALU.mult),
                                  waits=[t_br], sem="dv", inc=1)
        C.pb_free = t_xn_last
        C.sg_free[0] = t_xn_last
        C.up_free[0] = t_xn_last; C.up_free[1] = t_xn_last
        if parts == "ln":
            C.all_mm_done_dv = t_xn_last
            continue
        t_lg = None
        for tc in range(8):
            for k in range(8):
                t_lg = S.add("tensor", MM(p_g[:, tc * 8:(tc + 1) * 8],
                                          s_xn[:, k * TB + tc * 128: k * TB + tc * 128 + 128],
                                          s_rw[:, k * 8:(k + 1) * 8],
                                          k == 0, k == 7),
                             waits=([t_xn_last, C.gate_free.get(0), C.gate_free.get(1)]
                                    if (tc == 0 and k == 0) else ()),
                             sem="mm", inc=(1 if k == 7 else 0))
        t_ltm = S.add("vector", COPY(s_ltm, p_g[:, 0:64]), waits=[t_lg], sem="dv", inc=1)
        C.gate_free[0] = t_ltm
        if parts == "rtr" or parts == "ltm":
            C.all_mm_done_dv = t_ltm
            continue
        lt3 = s_ltm.rearrange("p (c e) -> p c e", e=8)
        def col3(ap):
            return ap.rearrange("p (c one) -> p c one", one=1)
        m1 = s_top[:, 0:8]; r1 = s_top[:, 8:16]; e1 = s_top[:, 16:24]
        m2 = s_top[:, 24:32]; r2 = s_top[:, 32:40]; e2 = s_top[:, 40:48]
        dd = s_top[:, 48:56]; w1 = s_top[:, 56:64]; w2 = s_top[:, 64:72]
        eq = s_top[:, 128:192]; eqb = s_top[:, 192:256]
        eq3 = eq.rearrange("p (c e) -> p c e", e=8)
        eqb3 = eqb.rearrange("p (c e) -> p c e", e=8)
        msk1 = sb["msk1"]; msk2 = sb["msk2"]; mskd = sb["mskd"]; t1m = sb["t1m"]; cwtm = sb["cwtm"]
        m13 = col3(m1); m23 = col3(m2)
        io3 = cn["iota8"].rearrange("p (c e) -> p c e", e=8)
        a = S.add
        a("vector", RED(m13, lt3, ALU.max), waits=[t_ltm])
        a("vector", TT(eq3, lt3, m13.broadcast_to([128, 8, 8]), ALU.is_ge))
        a("vector", TT(eqb, eq, cn["iota8r"], ALU.mult))
        a("vector", RED(col3(r1), eqb3, ALU.max))
        a("vector", TS(e1, r1, -1.0, 7.0, ALU.mult, ALU.add))
        a("vector", TT(msk1.rearrange("p (c e) -> p c e", e=8), io3,
                       col3(e1).broadcast_to([128, 8, 8]), ALU.is_equal))
        a("vector", STT(mskd, msk1, -1e30, s_ltm, ALU.mult, ALU.add))
        a("vector", RED(m23, mskd.rearrange("p (c e) -> p c e", e=8), ALU.max))
        a("vector", TT(eq3, mskd.rearrange("p (c e) -> p c e", e=8),
                       m23.broadcast_to([128, 8, 8]), ALU.is_ge))
        a("vector", TT(eqb, eq, cn["iota8r"], ALU.mult))
        a("vector", RED(col3(r2), eqb3, ALU.max))
        a("vector", TS(e2, r2, -1.0, 7.0, ALU.mult, ALU.add))
        a("vector", TT(msk2.rearrange("p (c e) -> p c e", e=8), io3,
                       col3(e2).broadcast_to([128, 8, 8]), ALU.is_equal))
        t_dd = a("vector", TT(dd, m1, m2, ALU.subtract), sem="dv", inc=1)
        t_w1 = a("scalar", ACTF(w1, dd, AF.Sigmoid, scale=1.0), waits=[t_dd], sem="act", inc=1)
        t_w2 = a("scalar", ACTF(w2, dd, AF.Sigmoid, scale=-1.0), waits=[t_dd], sem="act", inc=1)
        a("vector", TT(t1m.rearrange("p (c e) -> p c e", e=8), msk1.rearrange("p (c e) -> p c e", e=8),
                       col3(w1).broadcast_to([128, 8, 8]), ALU.mult), waits=[t_w1])
        a("vector", TT(cwtm.rearrange("p (c e) -> p c e", e=8), msk2.rearrange("p (c e) -> p c e", e=8),
                       col3(w2).broadcast_to([128, 8, 8]), ALU.mult), waits=[t_w2])
        t_cwtm = a("vector", TT(sb["cwr"], cwtm, t1m, ALU.add), sem="dv", inc=1)
        if parts == "top":
            C.all_mm_done_dv = t_cwtm
            continue
        p_cwt = p_b.bitcast(BF16)[0:64, 0:128]
        t_tr = S.add("tensor", TRANS(p_cwt, sb["cwr"], cn["id_bf"]),
                     waits=[t_cwtm], sem="mm", inc=1)
        t_cwe = S.add("vector", COPY(sb["cwT"], p_cwt), waits=[t_tr], sem="dv", inc=1)
        C.pb_free = t_cwe
        t_cw1 = S.add("sync", DMA(dws["cwscr"], sb["cwT"]),
                      waits=[t_cwe, C.cw_free], sem="cwd", inc=16)
        t_cw = S.add("sync", DMA(s_cw.rearrange("e (c t) -> e c t", t=128),
                                 dws["cwscr"].rearrange("(c e) t -> e c t", e=8)),
                     waits=[t_cw1], sem="cwd", inc=16)

        if parts == "pre":
            C.all_mm_done_dv = t_cw
            continue
        GW = 4 * 128
        NG = 8
        acc_tok = {}
        for e in range(ne):
            t_bc = []
            for s in range(2):
                t_bc.append(S.add("tensor", MM(p_b[:, s * 512:(s + 1) * 512],
                                               cn["E8"][:, e * 128:(e + 1) * 128],
                                               s_cw[:, s * 512:(s + 1) * 512], True, True),
                                  waits=[t_cw, C.pb_free, C.out_done.get((0, 0)),
                                         C.out_done.get((1, 0))], sem="mm", inc=1))
            t_xnup = None
            for k in range(8):
                for s in range(2):
                    t_xnup = S.add("vector", TT(
                        s_xnup[:, k * TB + s * 512: k * TB + (s + 1) * 512],
                        s_xn[:, k * TB + s * 512: k * TB + (s + 1) * 512],
                        p_b[:, s * 512:(s + 1) * 512], ALU.mult),
                        waits=[t_xn_last, t_bc[s]], sem="dv", inc=1)
            C.pb_free = t_xnup
            C.cw_free = t_bc[1]

            gu_dma = {}
            def ld_gu(g):
                par = g % 2
                tg = S.add("sync", DMA(s_wg[:, par * GW * 8:(par + 1) * GW * 8],
                                       dws["wgu"][e, 0, g].rearrange("p k i f -> p (k i f)")),
                           waits=[C.gu_free.get(par)], sem="wg%d" % par, inc=16)
                tu = S.add("sync", DMA(s_wu[:, par * GW * 8:(par + 1) * GW * 8],
                                       dws["wgu"][e, 1, g].rearrange("p k i f -> p (k i f)")),
                           sem="wu%d" % par, inc=16)
                gu_dma[g] = (tg, tu)
            wd_dma = {}
            def ld_wd(j):
                par = j % 2
                wd_dma[j] = S.add("sync", DMA(s_wd[:, par * 4096:(par + 1) * 4096],
                                              dws["wd"][e, j].rearrange("p i c -> p (i c)")),
                                  waits=[C.wd_free.get(par)], sem="wd%d" % par, inc=16)
            ld_gu(0); ld_gu(1)
            ld_wd(0); ld_wd(1)


            h_last = None
            for g in range(NG):
                par = g % 2
                base = par * GW * 8
                for ii in range(4):
                    i = g * 4 + ii
                    ph = i % 2
                    for s in range(2):
                        pg_h = p_g[:, ph * 512:(ph + 1) * 512]
                        pu_h = p_u[:, ph * 512:(ph + 1) * 512]
                        gate_w = [gu_dma[g][0], C.gate_free.get(ph), t_xn_last]
                        up_w = [gu_dma[g][1], C.up_free.get(ph), t_xnup]
                        tg = tu = None
                        for k in range(8):
                            wsl = s_wg[:, base + (k * 4 + ii) * 128: base + (k * 4 + ii) * 128 + 128]
                            tg = S.add("tensor", MM(pg_h, wsl,
                                                    s_xn[:, k * TB + s * 512: k * TB + (s + 1) * 512],
                                                    k == 0, k == 7),
                                       waits=(gate_w if k == 0 else ()), sem="mm", inc=(1 if k == 7 else 0))
                        for k in range(8):
                            wsl = s_wu[:, base + (k * 4 + ii) * 128: base + (k * 4 + ii) * 128 + 128]
                            tu = S.add("tensor", MM(pu_h, wsl,
                                                    s_xnup[:, k * TB + s * 512: k * TB + (s + 1) * 512],
                                                    k == 0, k == 7),
                                       waits=(up_w if k == 0 else ()), sem="mm", inc=(1 if k == 7 else 0))
                        sgh = s_sg[:, s * 512:(s + 1) * 512]
                        te = S.add("scalar", ACTF(sgh, pg_h, AF.Silu),
                                   waits=[tg, C.sg_free[s]], sem="act", inc=1)
                        C.gate_free[ph] = te
                        th = S.add("vector", TT(s_h[:, i * TB + s * 512: i * TB + (s + 1) * 512],
                                                sgh, pu_h, ALU.mult),
                                   waits=[te, tu], sem="dv", inc=1)
                        C.up_free[ph] = th
                        C.sg_free[s] = th
                        h_last = th
                C.gu_free[par] = S.tok("mm")
                if g + 2 < NG:
                    ld_gu(g + 2)
            for j in range(8):
                par = j % 2
                for s in range(2):
                    ph = s
                    pd_h = p_d[:, ph * 512:(ph + 1) * 512]
                    pre = [wd_dma[j], h_last, C.down_free.get(ph)]
                    td = None
                    for i in range(32):
                        wsl = s_wd[:, par * 4096 + i * 128: par * 4096 + i * 128 + 128]
                        td = S.add("tensor", MM(pd_h, wsl,
                                                s_h[:, i * TB + s * 512: i * TB + (s + 1) * 512],
                                                i == 0, i == 31),
                                   waits=(pre if i == 0 else ()), sem="mm", inc=(1 if i == 31 else 0))
                    accsl = s_acc[:, j * TB + s * 512: j * TB + (s + 1) * 512]
                    if e == 0:
                        ta = S.add("vector", COPY(accsl, pd_h),
                                   waits=[td, C.acc_free.get((j, s))], sem="dv", inc=1)
                    else:
                        ta = S.add("vector", TT(accsl, accsl, pd_h, ALU.add),
                                   waits=[td], sem="dv", inc=1)
                    C.down_free[ph] = ta
                    acc_tok[(j, s)] = ta
                C.wd_free[par] = S.tok("mm")
                if j + 2 < 8:
                    ld_wd(j + 2)

        C.prev_act = S.tok("act")
        all_done = acc_tok[(7, 1)]
        if parts == "exp":
            C.all_mm_done_dv = all_done
            continue
        prev_comb = None
        for s in range(2):
            t_xl = S.add("gpsimd", DMA(xv.rearrange("p (k t) -> p k t", k=8),
                                       dx.rearrange("(k p) t -> p k t", p=128)[:, :, t0 + s * 512:t0 + (s + 1) * 512]),
                         waits=[all_done, prev_comb], sem="xv", inc=16)
            t_sl = None
            if use_s:
                t_sl = S.add("gpsimd", DMA(sv.rearrange("p (k t) -> p k t", k=8),
                                           d_s[:, :, t0 + s * 512:t0 + (s + 1) * 512].rearrange("k p t -> p k t")),
                             waits=[all_done, prev_comb], sem="sv", inc=16)
            for j in range(8):
                accsl = s_acc[:, j * TB + s * 512: j * TB + (s + 1) * 512]
                ga = s_sg[:, 0:512]; gb = s_sg[:, 512:1024]
                tg_ = S.add("scalar", ACTF(ga, accsl, AF.Silu),
                            waits=[acc_tok[(j, s)], C.sg_free[0], prev_comb], sem="act", inc=1)
                C.acc_free[(j, s)] = tg_
                if use_s:
                    tm_ = S.add("vector", TT(gb, ga, sv[:, j * 512:(j + 1) * 512], ALU.mult),
                                waits=[tg_, t_sl, C.sg_free[1]], sem="dv", inc=1)
                    src = gb
                else:
                    tm_ = tg_
                    src = ga
                oslot = s_outb[:, (j % 2) * 512:((j % 2) + 1) * 512]
                to_ = S.add("vector", TT(oslot, src, xv[:, j * 512:(j + 1) * 512], ALU.add),
                            waits=[tm_, t_xl, C.out_done.get(((j % 2), 0))], sem="dv", inc=1)
                C.sg_free[0] = to_
                C.sg_free[1] = to_
                prev_comb = to_
                C.out_done[((j % 2), 0)] = S.add("scalar", DMA(d_out[j, :, t0 + s * 512:t0 + (s + 1) * 512],
                                                               oslot),
                                                 waits=[to_], sem="st%d" % (j % 2), inc=16)
        C.all_mm_done_dv = all_done
    return C




TA = 512
NBLKA = L // TA


def emit_attn(S, dx, d_s, dwa, sb, ps, cn, nblka=NBLKA, dbg=None, aparts="all"):
    a_x = sb["x"]; a_xn = sb["xn"]; a_z = sb["z"]
    a_cs = sb["cs"]; a_csp = sb["csp"]
    a_q = sb["q"]; a_k = sb["k"]; a_v = sb["v"]
    a_lo = sb["lo"]; a_mi = sb["mi"]
    otm = sb["otm"]
    a_feat = sb["feat"]; a_fT = sb["fT"]; a_ptm = sb["ptm"]
    a_h1 = sb["h1"]; a_mx = sb["mx"]; a_on = sb["on"]; a_onf = sb["onf"]
    a_sout = sb["z"]; a_stat = sb["stat"]; a_statr = sb["statr"]
    a_mb = sb["mb"]
    a_tl = sb["tl"]
    ch = sb["ch"]; a_sc = sb["sc"]; a_sbf = sb["sbf"]; a_dt = sb["dt"]
    a_dt2 = sb["dt2"]; a_xm = sb["xm"]; a_row = sb["row"]; a_sc2 = sb["sc2"]
    beta_tm = sb["beta"]
    arena = sb["arena"]
    rw2 = sb["rw2"]; wb_r = sb["wb"]; cvw = sb["cvw"]
    rb1 = sb["rb1"]; rb2r = sb["rb2"]; mixw = sb["mixw"]; onormw = sb["onorm"]

    a_sf = sb["sf32"]
    p_a, p_b2, p_c, p_d2 = ps["a"], ps["b2"], ps["c"], ps["d2"]

    CW_OFF = {"q": 0, "k": 4, "v": 8, "lo": 12, "mi": 19}

    X = {}

    def chsl(i):
        return ch[:, i * 512:(i + 1) * 512]

    t_ms = S.add("vector", (lambda e: e.memset(a_sbf, 0.0)), sem="dv", inc=1)
    for h in range(4):
        X[("sbf", h)] = t_ms
    S.add("vector", (lambda e: e.memset(a_tl, 0.0)), sem="dv", inc=1)
    S.add("vector", (lambda e: e.memset(a_feat, 0.0)), sem="dv", inc=1)
    S.add("vector", (lambda e: e.memset(a_sc2, 0.0)), sem="dv", inc=1)

    for blk in range(nblka):
        t0 = blk * TA
        wtok = {}

        def ld_w(m, extra=()):
            par = m % 2
            wtok[m] = S.add("sync", DMA(arena[:, par * 8192:(par + 1) * 8192].rearrange("p (k f) -> p k f", k=8),
                                        dwa["wqkv"][m]),
                            waits=[X.get(("wfree", par))] + list(extra), sem="wq%d" % par, inc=16)
        ld_w(0); ld_w(1)

        t_x = S.add("gpsimd", DMA(a_x.rearrange("p (k t) -> p k t", k=8),
                                  dx.rearrange("(k p) t -> p k t", p=128)[:, :, t0:t0 + TA]),
                    waits=[X.get("xfree")], sem="xa", inc=16)
        tok_k = {}; tok_ka = {}
        for k in range(8):
            par = k % 2
            xq = a_csp.bitcast(BF16)[:, par * TA:(par + 1) * TA]
            xk = a_x[:, k * TA:(k + 1) * TA]
            tok_ka[k] = S.add("scalar", ACTD(xq, xk, AF.Square),
                              waits=[t_x, tok_k.get(k - 2)] +
                                    ([X.get(("scanfree", 2)), X.get(("scanfree", 3))] if k == 0 else []),
                              sem="act", inc=1)
            S.add("tensor", MM(p_a[0:1, 0:TA], cn["ones_bf"], xk, k == 0, k == 7),
                  waits=([t_x, X.get("pa_free")] if k == 0 else []))
            tok_k[k] = S.add("tensor", MM(p_a[32:33, 0:TA], cn["ones_bf"], xq, k == 0, k == 7),
                             waits=[tok_ka[k]], sem="mm", inc=1)
        mu_ = a_stat[0:1, 0:TA]; var_ = a_stat[0:1, TA:2 * TA]
        rrow = a_statr[0:1, 0:TA]
        t_mu = S.add("vector", TS(mu_, p_a[0:1, 0:TA], 1.0 / D, None, ALU.mult),
                     waits=[tok_k[7], X.get("statact")], sem="dv", inc=1)
        t_mur = S.add("vector", COPY(rrow, mu_), waits=[t_mu], sem="dv", inc=1)
        t_bm = S.add("tensor", MM(p_b2[:, 0:TA], cn["ones_row"], rrow, True, True),
                     waits=[t_mur, X.get("pb2_free")], sem="mm", inc=1)
        S.add("vector", TT(var_, mu_, mu_, ALU.mult), waits=[t_mur])
        S.add("vector", STT(mu_, p_a[32:33, 0:TA], 1.0 / D, var_, ALU.mult, ALU.subtract))
        S.add("vector", TS(var_, mu_, 1e-5, None, ALU.add))
        t_inv = S.add("vector", RECIP(mu_, var_), sem="dv", inc=1)
        X["pa_free"] = t_inv
        t_rsr = S.add("scalar", ACTD(rrow, mu_, AF.Sqrt), waits=[t_inv, t_bm], sem="act", inc=1)
        X["statact"] = t_rsr
        t_br = S.add("tensor", MM(p_c[:, 0:TA], cn["ones_row"], rrow, True, True),
                     waits=[t_rsr, X.get("pc_free")], sem="mm", inc=1)
        t_xn = None
        for k in range(8):
            scr = a_sc[:, 0:TA]
            S.add("vector", TT(scr, a_x[:, k * TA:(k + 1) * TA], p_b2[:, 0:TA], ALU.subtract),
                  waits=[t_bm, X.get("scact")])
            t_xn = S.add("vector", TT(a_xn[:, k * TA:(k + 1) * TA], scr, p_c[:, 0:TA], ALU.mult),
                         waits=[t_br], sem="dv", inc=1)
        X["xfree"] = t_xn
        X["pb2_free"] = t_xn
        X["pc_free"] = t_xn

        if aparts == "ln":
            continue
        t_bmm = None
        for tc in range(4):
            for k in range(8):
                t_bmm = S.add("tensor", MM(p_d2[:, tc * 4:(tc + 1) * 4],
                                           a_xn[:, k * TA + tc * 128: k * TA + tc * 128 + 128],
                                           wb_r[:, k * 4:(k + 1) * 4], k == 0, k == 7),
                              waits=([t_xn, X.get("pd2_free")] if (tc == 0 and k == 0) else ()),
                              sem="mm", inc=(1 if k == 7 else 0))
        t_beta = S.add("scalar", ACTF(beta_tm, p_d2[:, 0:16], AF.Sigmoid), waits=[t_bmm], sem="act", inc=1)
        X["pd2_free"] = t_beta

        outdst = {0: a_q, 1: a_k, 2: a_v}
        for m, nm in [(0, "q"), (1, "k"), (2, "v")]:
            for c in range(8):
                par = c % 2
                t_zm = None
                for k in range(8):
                    wsl = arena[:, (m % 2) * 8192 + k * 1024 + c * 128:(m % 2) * 8192 + k * 1024 + c * 128 + 128]
                    t_zm = S.add("tensor", MM(p_a[:, 0:512], wsl, a_xn[:, k * TA:(k + 1) * TA],
                                              k == 0, k == 7),
                                 waits=([wtok[m], X.get("pa_free"), t_xn] if k == 0 else ()),
                                 sem="mm", inc=(1 if k == 7 else 0))
                zt = a_z.bitcast(BF16)[:, par * 515:par * 515 + 515]
                tl = a_tl[:, (m * 8 + c) * 3:(m * 8 + c) * 3 + 3]
                S.add("vector", COPY(zt[:, 0:3], tl),
                      waits=[X.get(("ztl", m, c))] + ([X.get(("sof", 0)), X.get(("sof", 1))]
                                                      if (m == 0 and c == 0) else []))
                t_ze = S.add("vector", COPY(zt[:, 3:515], p_a[:, 0:512]), waits=[t_zm], sem="dv", inc=1)
                X["pa_free"] = t_ze
                t_ts = S.add("vector", COPY(tl, zt[:, 512:515]), sem="dv", inc=1)
                X[("ztl", m, c)] = t_ts
                sA = a_sc.bitcast(BF16)[:, 0:512]; sB = a_sc.bitcast(BF16)[:, 512:1024]
                w0 = cvw[:, c * 44 + CW_OFF[nm]:c * 44 + CW_OFF[nm] + 1]
                S.add("vector", TS(sA, zt[:, 0:512], w0, None, ALU.mult), waits=[X.get("scact")])
                for tau in range(1, 4):
                    wt = cvw[:, c * 44 + CW_OFF[nm] + tau:c * 44 + CW_OFF[nm] + tau + 1]
                    src, dst = (sA, sB) if tau % 2 == 1 else (sB, sA)
                    last = tau == 3
                    S.add("vector", STT(dst, zt[:, tau:tau + 512], wt, src, ALU.mult, ALU.add),
                          sem="dv", inc=(1 if last else 0))
                t_cv = S.tok("dv")
                off = c * 640 + 128 if m == 2 else c * 512
                t_si = S.add("scalar", ACTD(outdst[m][:, off:off + 512], sB, AF.Silu),
                             waits=[t_cv], sem="act", inc=1)
                X["scact"] = t_si
                X[("qkv", m, c)] = t_si
            X[("wfree", m % 2)] = S.tok("mm")
            if m == 0:
                ld_w(2)

        if aparts == "qkv":
            continue
        vt_tok = {}
        for c in range(8):
            vt = a_v[:, c * 640:c * 640 + 640]
            tl = a_tl[:, 72 + c * 24:72 + c * 24 + 24]
            t1 = S.add("vector", COPY(vt[:, 104:128], tl),
                       waits=[X.get(("vtl", c)), X.get(("qkv", 2, c))], sem="dv", inc=1)
            t2 = S.add("vector", COPY(tl, vt[:, 616:640]), sem="dv", inc=1)
            X[("vtl", c)] = t2
            vt_tok[c] = t1

        for kind, nm, K in [(0, "lo", 7), (1, "mi", 25)]:
            for c in range(8):
                vt = a_v[:, c * 640:c * 640 + 640]
                eng = "vector"
                sA = a_sc.bitcast(BF16)[:, 0:512]; sB = a_sc.bitcast(BF16)[:, 512:1024]
                wpre = [X.get("scact"), vt_tok[c], X.get(("vtl", c))]
                base = 128 - (K - 1)
                w0 = cvw[:, c * 44 + CW_OFF[nm]:c * 44 + CW_OFF[nm] + 1]
                S.add(eng, TS(sA, vt[:, base:base + 512], w0, None, ALU.mult), waits=wpre)
                for tau in range(1, K):
                    wt = cvw[:, c * 44 + CW_OFF[nm] + tau:c * 44 + CW_OFF[nm] + tau + 1]
                    src, dst = (sA, sB) if tau % 2 == 1 else (sB, sA)
                    S.add(eng, STT(dst, vt[:, base + tau:base + tau + 512], wt, src, ALU.mult, ALU.add))
                fin = sB if (K - 1) % 2 == 1 else sA
                dstt = (a_lo if kind == 0 else a_mi)[:, (c % 2) * 512:(c % 2) * 512 + 512]
                t_f = S.add(eng, COPY(dstt, fin), waits=[X.get(("losl", kind, c % 2))],
                            sem=("dv" if eng == "vector" else "gp"), inc=1)
                if eng == "vector":
                    X["scact"] = t_f
                h, cc = c // 2, c % 2
                for tc in range(4 if aparts != "lomicv" else 0):
                    insl = dstt[:, tc * 128:tc * 128 + 128]
                    slot = tc % 4
                    pt = p_c.bitcast(BF16)[:, slot * 128:slot * 128 + 128]
                    t_t = S.add("tensor", TRANS(pt, insl, cn["id_bf"]),
                                waits=[t_f, X.get("ptrl"), X.get(("ptr", slot)), X.get("pc_free")], sem="mm", inc=1)
                    dsl = otm[:, kind * 4096 + tc * 1024 + h * 256 + cc * 128:
                              kind * 4096 + tc * 1024 + h * 256 + cc * 128 + 128]
                    t_e = S.add("vector", COPY(dsl, pt), waits=[t_t], sem="dv", inc=1)
                    X[("ptr", slot)] = t_e
                    X["ptrl"] = t_e
                    X[("otm", kind, tc, h, cc)] = t_e
                X[("losl", kind, c % 2)] = S.tok("mm")
        for c in range(8):
            h, cc = c // 2, c % 2
            for tc in range(4):
                insl = a_v[:, c * 640 + 128 + tc * 128:c * 640 + 128 + tc * 128 + 128]
                slot = tc % 4
                pt = p_c.bitcast(BF16)[:, slot * 128:slot * 128 + 128]
                t_t = S.add("tensor", TRANS(pt, insl, cn["id_bf"]),
                            waits=[X.get(("qkv", 2, c)), X.get("ptrl"), X.get(("ptr", slot)), X.get("pc_free")],
                            sem="mm", inc=1)
                dsl = otm[:, 3 * 4096 + tc * 1024 + h * 256 + cc * 128:
                          3 * 4096 + tc * 1024 + h * 256 + cc * 128 + 128]
                t_e = S.add("vector", COPY(dsl, pt), waits=[t_t], sem="dv", inc=1)
                X[("ptr", slot)] = t_e
                X["ptrl"] = t_e
                X[("otm", 3, tc, h, cc)] = t_e

        if dbg is not None and blk == 0 and aparts != "lomicv":
            if True:
                wv = [X[("otm", 3, tcx, hx, ccx)] for tcx in range(4) for hx in range(4) for ccx in range(2)]
                S.add("gpsimd", DMA(dbg["vd0"], otm[:, 3 * 4096:4 * 4096]), waits=wv, sem="dbg", inc=16)
                S.add("vector", NOP(), waits=[S.tok("dbg")], sem="dv", inc=1)
        if aparts in ("lomi", "lomicv"):
            continue
        p_e, p_f, p_g2, p_h2 = ps["e"], ps["f"], ps["g2"], ps["h2"]
        scanps = [p_e, p_f, p_g2, p_h2]
        mstsl = lambda g_: chsl(10 + g_ % 2)

        def bc128(ap):
            return ap.rearrange("p (one f) -> p one f", one=1).broadcast_to([128, 4, 128])

        def v3(sl):
            return chsl(sl).rearrange("p (hh f) -> p hh f", f=128)

        def rr(gens):
            gens = list(gens)
            while gens:
                nxt = []
                for gg in gens:
                    try:
                        next(gg)
                        nxt.append(gg)
                    except StopIteration:
                        pass
                gens = nxt

        scl_tok = {}

        def gen_prep(h, g):
            par = h % 2
            ktm = a_dt[:, h * 256:(h + 1) * 256]
            kl2 = a_dt2[:, (g % 2) * 1024 + h * 256:(g % 2) * 1024 + h * 256 + 256]
            scr = a_sc.bitcast(BF16)[:, par * 256:(par + 1) * 256]
            rb = 256 + par * 4
            rkraw = a_sc[:, rb:rb + 1]; rkc = a_sc[:, rb + 1:rb + 2]
            brk = a_sc[:, rb + 2:rb + 3]; rki = a_sc[:, rb + 3:rb + 4]
            qraw = a_sc[0:1, 264 + par * 128:392 + par * 128]
            qinv = a_sc[0:1, 520 + par * 128:648 + par * 128]
            rowb = a_row[0:1, par * 384:par * 384 + 384]
            xmb = a_xm[:, par * 384:par * 384 + 384]
            p_go = p_a if par == 0 else p_b2
            p_sm = p_d2 if par == 0 else p_c
            smo = 0 if par == 0 else 128
            csb = a_cs.bitcast(BF16)[:, par * 256:(par + 1) * 256]
            bcol = beta_tm[:, g * 4 + h:g * 4 + h + 1]
            rqcol = a_sc2[:, 4 + (g % 2) * 4 + h:5 + (g % 2) * 4 + h]
            for cc in range(2):
                insl = a_k[:, (2 * h + cc) * TA + g * 128:(2 * h + cc) * TA + g * 128 + 128]
                pt = p_c.bitcast(BF16)[:, cc * 128:cc * 128 + 128]
                t_t = S.add("tensor", TRANS(pt, insl, cn["id_bf"]),
                            waits=[X.get(("qkv", 1, 2 * h + cc)), X.get("ptrl"), X.get(("ptr", cc))],
                            sem="mm", inc=1)
                tke = S.add("vector", COPY(ktm[:, cc * 128:cc * 128 + 128], pt), waits=[t_t], sem="dv", inc=1)
                X[("ptr", cc)] = tke
                X["ptrl"] = tke
            yield
            S.add("vector", TT(scr, ktm, ktm, ALU.mult), waits=[X.get("scact"), X.get(("scact", par))])
            S.add("vector", RED(rkraw.rearrange("p (a one) -> p a one", one=1),
                                scr.rearrange("p (a x) -> p a x", x=256), ALU.add))
            S.add("vector", TS(rkraw, rkraw, 1e-12, None, ALU.add))
            t_i1 = S.add("vector", RECIP(rki, rkraw), sem="dv", inc=1)
            yield
            t_rk = S.add("scalar", ACTF(rkc, rki, AF.Sqrt), waits=[t_i1], sem="act", inc=1)
            yield
            t_brk = S.add("vector", TT(brk, rkc, bcol, ALU.mult), waits=[t_rk, t_beta], sem="dv", inc=1)
            t_kb = S.add("vector", TS(kl2, ktm, rkc, None, ALU.mult),
                         waits=[t_rk, X.get(("dt2", h))], sem="dv", inc=1)
            scl_tok[h] = t_kb
            t_sqa = {}
            for cc in range(2):
                qs = a_q[:, (2 * h + cc) * TA + g * 128:(2 * h + cc) * TA + g * 128 + 128]
                t_sqa[cc] = S.add("scalar", ACTD(csb[:, cc * 128:cc * 128 + 128], qs, AF.Square),
                                  waits=[X.get(("qkv", 0, 2 * h + cc)), X.get(("csq", par))], sem="act", inc=1)
            yield
            t_qmm = None
            for cc in range(2):
                t_qmm = S.add("tensor", MM(p_sm[0:1, smo:smo + 128], cn["ones_bf"],
                                           csb[:, cc * 128:cc * 128 + 128], cc == 0, cc == 1),
                              waits=([t_sqa[0], t_sqa[1]] +
                                     ([X.get("pd2_free")] if par == 0 else [])
                                     if cc == 0 else ()),
                              sem="mm", inc=(1 if cc else 0))
            t_qrow = S.add("vector", COPY(qraw, p_sm[0:1, smo:smo + 128]), waits=[t_qmm], sem="dv", inc=1)
            X[("csq", par)] = t_qrow
            S.add("vector", TS(qraw, qraw, 1e-12, None, ALU.add))
            t_i2 = S.add("vector", RECIP(qinv, qraw), sem="dv", inc=1)
            yield
            t_rqr = S.add("scalar", ACTF(rowb[0:1, 256:384], qinv, AF.Sqrt), waits=[t_i2], sem="act", inc=1)
            X[("scact", par)] = t_rqr
            yield
            t_rqc = S.add("tensor", MMD(p_sm[:, smo + 128:smo + 129], rowb[0:1, 256:384],
                                        cn["ones_row"][0:1, 0:1], True, True),
                          waits=[t_rqr], sem="mm", inc=1)
            t_rqcol = S.add("vector", COPY(rqcol, p_sm[:, smo + 128:smo + 129]), waits=[t_rqc], sem="dv", inc=1)
            t_ckb = S.add("vector", COPY(a_dt[:, 1024 + par * 2:1025 + par * 2], rkc), sem="dv", inc=1)
            t_ckb2 = S.add("vector", COPY(a_dt[:, 1025 + par * 2:1026 + par * 2], brk), waits=[t_brk],
                           sem="dv", inc=1)
            yield
            t_rkrow = S.add("tensor", MM(p_sm[0:1, smo + 136:smo + 264], a_dt[:, 1024 + par * 2:1025 + par * 2],
                                         cn["id_bf"], True, True),
                            waits=[t_ckb], sem="mm", inc=1)
            t_brkrow = S.add("tensor", MM(p_sm[32:33, smo + 136:smo + 264], a_dt[:, 1025 + par * 2:1026 + par * 2],
                                          cn["id_bf"], True, True),
                             waits=[t_ckb2], sem="mm", inc=1)
            t_r1 = S.add("vector", COPY(rowb[0:1, 0:128], p_sm[0:1, smo + 136:smo + 264]),
                         waits=[t_rkrow], sem="dv", inc=1)
            t_r2 = S.add("vector", COPY(rowb[0:1, 128:256], p_sm[32:33, smo + 136:smo + 264]),
                         waits=[t_brkrow], sem="dv", inc=1)
            if par == 0:
                X["pd2_free"] = t_r2
                X["pd2o"] = t_r2
            else:
                X["pc_free"] = t_r2
            yield
            t_g = None
            for cc in range(2):
                ksl = a_k[:, (2 * h + cc) * TA + g * 128:(2 * h + cc) * TA + g * 128 + 128]
                t_g = S.add("tensor", MM(p_go[:, 0:128], ksl, ksl, cc == 0, cc == 1),
                            waits=([X.get("pa_free") if par == 0 else X.get("pb2_free"),
                                    X.get(("qkv", 1, 2 * h)), X.get(("qkv", 1, 2 * h + 1))]
                                   if cc == 0 else ()),
                            sem="mm", inc=(1 if cc else 0))
            t_kq = None
            for cc in range(2):
                ksl = a_k[:, (2 * h + cc) * TA + g * 128:(2 * h + cc) * TA + g * 128 + 128]
                qsl = a_q[:, (2 * h + cc) * TA + g * 128:(2 * h + cc) * TA + g * 128 + 128]
                t_kq = S.add("tensor", MM(p_go[:, 128:256], ksl, qsl, cc == 0, cc == 1),
                             waits=([X.get(("qkv", 0, 2 * h)), X.get(("qkv", 0, 2 * h + 1))] if cc == 0 else ()),
                             sem="mm", inc=(1 if cc else 0))
            yield
            t_o1 = S.add("tensor", MMD(p_go[:, 256:384], rowb[0:1, 128:256], rowb[0:1, 0:128], True, True),
                         waits=[t_r1, t_r2], sem="mm", inc=1)
            t_x1 = S.add("vector", TT(xmb[:, 0:128], p_go[:, 256:384], cn["mkA"], ALU.mult),
                         waits=[t_o1], sem="dv", inc=1)
            yield
            t_o2 = S.add("tensor", MMD(p_go[:, 384:512], rowb[0:1, 0:128], rowb[0:1, 128:256], True, True),
                         waits=[t_x1], sem="mm", inc=1)
            t_x2 = S.add("vector", TT(xmb[:, 128:256], p_go[:, 384:512], cn["mkAu"], ALU.mult),
                         waits=[t_o2], sem="dv", inc=1)
            yield
            t_o3 = S.add("tensor", MMD(p_go[:, 256:384], rowb[0:1, 0:128], cn["ones_row"], True, True),
                         waits=[t_x2], sem="mm", inc=1)
            t_x3 = S.add("vector", TT(xmb[:, 256:384], p_go[:, 256:384], cn["mkT"], ALU.mult),
                         waits=[t_o3], sem="dv", inc=1)
            if dbg is not None and blk == 0 and g == 0 and h == 0:
                S.add("gpsimd", DMA(dbg["row"], a_row[0:1, 0:384]), waits=[t_r2, t_rqb], sem="dbg", inc=16)
                S.add("gpsimd", DMA(dbg["xm"], a_xm[:, 0:384]), waits=[t_x3], sem="dbg", inc=16)
                S.add("vector", NOP(), waits=[S.tok("dbg")], sem="dv", inc=1)
            yield
            t_A = S.add("vector", TT(chsl(0)[:, h * 128:(h + 1) * 128], p_go[:, 0:128], xmb[:, 0:128], ALU.mult),
                        waits=[t_g, t_x1, X.get("chainmm")], sem="dv", inc=1)
            t_Ab = S.add("vector", TT(chsl(1)[:, h * 128:(h + 1) * 128], p_go[:, 0:128], xmb[:, 128:256],
                                      ALU.mult),
                         waits=[t_x2], sem="dv", inc=1)
            t_aT = S.add("vector", TT(chsl(8 + g % 2)[:, h * 128:(h + 1) * 128], p_go[:, 128:256],
                                      xmb[:, 256:384], ALU.mult),
                         waits=[t_kq, t_x3], sem="dv", inc=1)
            if par == 0:
                X["pa_free"] = t_aT
            else:
                X["pb2_free"] = t_aT
            S.add("vector", TT(chsl(2)[:, h * 128:(h + 1) * 128], chsl(0)[:, h * 128:(h + 1) * 128],
                               cn["id_bf"], ALU.add), waits=[t_A])
            S.add("vector", TT(chsl(3)[:, h * 128:(h + 1) * 128], chsl(1)[:, h * 128:(h + 1) * 128],
                               cn["id_bf"], ALU.add), waits=[t_Ab])

        def gen_chain(g):
            tok_prev = S.tok("dv")
            S.add("vector", TT(v3(4), v3(1), bc128(cn["mD16u"]), ALU.mult), waits=[tok_prev])
            S.add("vector", TT(v3(5), v3(0), bc128(cn["mD16l"]), ALU.mult))
            S.add("vector", TT(v3(3), v3(4), bc128(cn["id_bf"]), ALU.add))
            t_base = S.add("vector", TT(v3(2), v3(5), bc128(cn["id_bf"]), ALU.add), sem="dv", inc=1)
            tok_prev = t_base
            yield
            for j in range(6):
                lvlmask = None if j < 3 else ("mB16", "mB32", "mB64")[j - 3]
                if j < 3:
                    tq = tr = None
                    for h in range(4):
                        tq = S.add("tensor", MM(p_a[:, h * 128:(h + 1) * 128],
                                                chsl(5)[:, h * 128:(h + 1) * 128],
                                                chsl(4)[:, h * 128:(h + 1) * 128], True, True),
                                   waits=([tok_prev, X.get("pa_free")] if h == 0 else ()),
                                   sem="mm", inc=(1 if h == 3 else 0))
                    for h in range(4):
                        tr = S.add("tensor", MM(p_c[:, h * 128:(h + 1) * 128],
                                                chsl(4)[:, h * 128:(h + 1) * 128],
                                                chsl(5)[:, h * 128:(h + 1) * 128], True, True),
                                   waits=([tok_prev, X.get("pc_free")] if h == 0 else ()),
                                   sem="mm", inc=(1 if h == 3 else 0))
                    yield
                    t_q = S.add("vector", COPY(chsl(4), p_a[:, 0:512]), waits=[tq, tr], sem="dv", inc=1)
                    t_r = S.add("vector", COPY(chsl(5), p_c[:, 0:512]), waits=[tr], sem="dv", inc=1)
                else:
                    t_q = S.add("vector", TT(v3(4), v3(1), bc128(cn[lvlmask + "u"]), ALU.mult),
                                waits=[tok_prev], sem="dv", inc=1)
                    t_r = S.add("vector", TT(v3(5), v3(0), bc128(cn[lvlmask + "l"]), ALU.mult),
                                sem="dv", inc=1)
                yield
                ty = tz = None
                for h in range(4):
                    ty = S.add("tensor", MM(p_a[:, h * 128:(h + 1) * 128],
                                            chsl(5)[:, h * 128:(h + 1) * 128],
                                            chsl(3)[:, h * 128:(h + 1) * 128], True, True),
                               waits=([t_q, t_r, X.get("pa_free")] if h == 0 else ()),
                               sem="mm", inc=(1 if h == 3 else 0))
                for h in range(4):
                    tz = S.add("tensor", MM(p_b2[:, h * 128:(h + 1) * 128],
                                            chsl(4)[:, h * 128:(h + 1) * 128],
                                            chsl(2)[:, h * 128:(h + 1) * 128], True, True),
                               waits=([t_q, t_r, X.get("pb2_free")] if h == 0 else ()),
                               sem="mm", inc=(1 if h == 3 else 0))
                yield
                t_y = S.add("vector", COPY(chsl(6), p_a[:, 0:512]), waits=[ty], sem="dv", inc=1)
                t_z = S.add("vector", COPY(chsl(7), p_b2[:, 0:512]), waits=[tz], sem="dv", inc=1)
                yield
                tx = txp = None
                for h in range(4):
                    tx = S.add("tensor", MM(p_a[:, h * 128:(h + 1) * 128],
                                            chsl(2)[:, h * 128:(h + 1) * 128],
                                            chsl(6)[:, h * 128:(h + 1) * 128], True, True),
                               waits=([t_y, t_z] if h == 0 else ()), sem="mm", inc=(1 if h == 3 else 0))
                for h in range(4):
                    txp = S.add("tensor", MM(p_b2[:, h * 128:(h + 1) * 128],
                                             chsl(3)[:, h * 128:(h + 1) * 128],
                                             chsl(7)[:, h * 128:(h + 1) * 128], True, True),
                                waits=([t_y, t_z] if h == 0 else ()), sem="mm", inc=(1 if h == 3 else 0))
                yield
                mdst = mstsl(g) if j == 5 else chsl(3)
                t_m = S.add("vector", TT(mdst, chsl(3), p_a[:, 0:512], ALU.add), waits=[tx, txp],
                            sem="dv", inc=1)
                t_p = S.add("vector", TT(chsl(2), chsl(2), p_b2[:, 0:512], ALU.add), waits=[txp],
                            sem="dv", inc=1)
                X["pa_free"] = t_m
                X["pb2_free"] = t_p
                X["pc_free"] = t_r
                tok_prev = t_p
                yield
            X[("chaindone", g)] = t_m
            if dbg is not None and blk == 0 and g == 0:
                S.add("gpsimd", DMA(dbg["M"], mstsl(0)), waits=[t_m], sem="dbg", inc=16)
                S.add("vector", NOP(), waits=[S.tok("dbg")], sem="dv", inc=1)

        def gen_scan_h(g, h):
            pw = scanps[h]
            kl2 = a_dt2[:, (g % 2) * 1024 + h * 256:(g % 2) * 1024 + h * 256 + 256]
            Msl = mstsl(g)[:, h * 128:(h + 1) * 128]
            aTsl = chsl(8 + g % 2)[:, h * 128:(h + 1) * 128]
            mbs = a_mb[:, h * 128:(h + 1) * 128]
            bcol = beta_tm[:, g * 4 + h:g * 4 + h + 1]
            vdsl = otm[:, 3 * 4096 + g * 1024 + h * 256:3 * 4096 + g * 1024 + h * 256 + 256]
            csb2 = a_csp.bitcast(BF16)
            wts = [a_dt[:, 1056:1312], a_dt[:, 1568:1824], csb2[:, 0:256], csb2[:, 512:768]][h]
            ups = [a_dt[:, 1312:1568], a_dt[:, 1824:2080], csb2[:, 256:512], csb2[:, 768:1024]][h]
            wss = a_mx[:, h * 256:(h + 1) * 256]
            t_mb = S.add("vector", TS(mbs, Msl, bcol, None, ALU.mult),
                         waits=[X.get(("chaindone", g)), X.get(("scanfree", h)), t_beta], sem="dv", inc=1)
            t_wt = None
            for cc in range(2):
                t_wt = S.add("tensor", MM(pw[:, 256 + cc * 128:256 + cc * 128 + 128],
                                          kl2[:, cc * 128:(cc + 1) * 128], mbs, True, True),
                             waits=([t_mb] if cc == 0 else ()),
                             sem="mm", inc=(1 if cc else 0))
            t_wte = S.add("vector", COPY(wts, pw[:, 256:512]), waits=[t_wt], sem="dv", inc=1)
            yield
            t_ws = None
            for cc in range(2):
                t_ws = S.add("tensor", MM(pw[:, 0:256], wts[:, cc * 128:cc * 128 + 128],
                                          a_sbf[:, h * 512 + cc * 256:h * 512 + cc * 256 + 256],
                                          cc == 0, cc == 1),
                             waits=([t_wte, X.get(("sbf", h))] if cc == 0 else ()),
                             sem="mm", inc=(1 if cc else 0))
            t_wse = S.add("vector", COPY(wss, pw[:, 0:256]), waits=[t_ws], sem="dv", inc=1)
            yield
            t_u = S.add("tensor", MM(pw[:, 0:256], mbs, vdsl, True, True),
                        waits=[t_wse, X.get(("otm", 3, g, h, 0)), X.get(("otm", 3, g, h, 1))],
                        sem="mm", inc=1)
            t_up = S.add("vector", TT(ups, pw[:, 0:256], wss, ALU.subtract), waits=[t_u], sem="dv", inc=1)
            yield
            t_om = None
            for cc in range(2):
                qsl = a_q[:, (2 * h + cc) * TA + g * 128:(2 * h + cc) * TA + g * 128 + 128]
                t_om = S.add("tensor", MM(pw[:, 256:512], qsl,
                                          a_sbf[:, h * 512 + cc * 256:h * 512 + cc * 256 + 256],
                                          cc == 0, False),
                             waits=([t_up] if cc == 0 else ()), sem="mm", inc=0)
            t_om = S.add("tensor", MM(pw[:, 256:512], aTsl, ups, False, True), sem="mm", inc=1)
            dsl = otm[:, 2 * 4096 + g * 1024 + h * 256:2 * 4096 + g * 1024 + h * 256 + 256]
            t_oe = S.add("vector", TS(dsl, pw[:, 256:512],
                                      a_sc2[:, 4 + (g % 2) * 4 + h:5 + (g % 2) * 4 + h], None, ALU.mult),
                         waits=[t_om], sem="dv", inc=1)
            X[("otm", 2, g, h, 0)] = t_oe
            X[("otm", 2, g, h, 1)] = t_oe
            yield
            t_su = None
            for cc in range(2):
                t_su = S.add("tensor", MM(pw[:, cc * 256:(cc + 1) * 256],
                                          kl2[:, cc * 128:(cc + 1) * 128], ups, True, True),
                             waits=([t_oe] if cc == 0 else ()), sem="mm", inc=(1 if cc else 0))
            for cc in range(2):
                sfsl = a_sf[:, h * 512 + cc * 256:h * 512 + (cc + 1) * 256]
                if g == 0 and blk == 0:
                    S.add("vector", COPY(sfsl, pw[:, cc * 256:(cc + 1) * 256]), waits=[t_su])
                else:
                    S.add("vector", TT(sfsl, sfsl, pw[:, cc * 256:(cc + 1) * 256], ALU.add), waits=[t_su])
            t_se = S.add("vector", COPY(a_sbf[:, h * 512:(h + 1) * 512], a_sf[:, h * 512:(h + 1) * 512]),
                         sem="dv", inc=1)
            X[("sbf", h)] = t_se
            X[("scanfree", h)] = t_se
            X[("dt2", h)] = t_se
            if dbg is not None and blk == 0 and g == 0 and h == 0:
                S.add("gpsimd", DMA(dbg["wt"], a_dt[:, 1056:1312]), waits=[t_wte], sem="dbg", inc=16)
                S.add("gpsimd", DMA(dbg["up"], a_dt[:, 1312:1568]), waits=[t_up], sem="dbg", inc=16)
                S.add("gpsimd", DMA(dbg["sb0"], a_sbf[:, 0:512]), waits=[t_se], sem="dbg", inc=16)
                S.add("vector", NOP(), waits=[S.tok("dbg")], sem="dv", inc=1)

        for g in range(4):
            gens = ([gen_scan_h(g - 1, hh) for hh in range(4)] if g > 0 else [])
            rr(gens + [gen_prep(0, g), gen_prep(1, g)])
            rr([gen_prep(2, g), gen_prep(3, g)])
            if dbg is not None and blk == 0 and g == 0:
                wd_ = [scl_tok[hh] for hh in range(4)]
                S.add("gpsimd", DMA(dbg["ch0"], ch[:, 0:512]), waits=wd_ + [S.tok("dv")], sem="dbg", inc=16)
                S.add("gpsimd", DMA(dbg["ch1"], ch[:, 512:1024]), waits=wd_, sem="dbg", inc=16)
                S.add("gpsimd", DMA(dbg["ch8"], ch[:, 8 * 512:9 * 512]), waits=wd_, sem="dbg", inc=16)
                S.add("gpsimd", DMA(dbg["dt2"], a_dt2[:, 0:2048]), waits=wd_, sem="dbg", inc=16)
                S.add("gpsimd", DMA(dbg["sc2"], a_sc2[:, 0:8]), waits=wd_, sem="dbg", inc=16)
                S.add("vector", NOP(), waits=[S.tok("dbg")], sem="dv", inc=1)
            rr([gen_chain(g)])
            X["chainmm"] = S.tok("mm")
        rr([gen_scan_h(3, hh) for hh in range(4)])

        if aparts == "delta":
            continue
        t_feat = None
        for tc in range(4):
            fr = a_feat[:, tc * 64:(tc + 1) * 64]
            scr = a_sc.bitcast(BF16)[:, 0:1024]
            wpre = [X.get(("otm", kk, tc, hh, cc2)) for kk in range(4) for hh in range(4) for cc2 in range(2)]
            osl = lambda kk: otm[:, kk * 4096 + tc * 1024:kk * 4096 + tc * 1024 + 1024]
            o3 = lambda kk: osl(kk).rearrange("p (h d) -> p h d", d=256)
            f3 = lambda f: fr[:, f * 4:(f + 1) * 4].rearrange("p (h one) -> p h one", one=1)
            S.add("vector", RED(f3(0), o3(0), ALU.add), waits=wpre + [X.get("scact")])
            for i in range(1, 4):
                S.add("vector", RED(f3(2 * i), o3(i), ALU.add))
            for i in range(4):
                S.add("vector", TT(scr, osl(i), osl(i), ALU.mult))
                S.add("vector", RED(f3(2 * i + 1), scr.rearrange("p (h d) -> p h d", d=256), ALU.add))
            fidx = 8
            for i in range(4):
                for jj in range(i + 1, 4):
                    S.add("vector", TT(scr, osl(i), osl(jj), ALU.mult))
                    S.add("vector", RED(f3(fidx), scr.rearrange("p (h d) -> p h d", d=256), ALU.add))
                    fidx += 1
            t_feat = S.add("vector", (lambda e: e.nop()), sem="dv", inc=1)
        S.add("vector", TT(a_feat, a_feat, cn["fscale"], ALU.mult), waits=[t_feat])
        scr16 = a_sc[:, 0:64]
        for tcq in range(4):
            fv = a_feat[:, tcq * 64:(tcq + 1) * 64].rearrange("p (f h) -> p f h", h=4)
            sv16 = scr16[:, tcq * 16:(tcq + 1) * 16].rearrange("p (i h) -> p i h", h=4)
            S.add("vector", (lambda e, fv=fv, sv16=sv16: e.tensor_tensor(
                out=sv16, in0=fv[:, 0:8:2, :], in1=fv[:, 0:8:2, :], op=ALU.mult)))
            S.add("vector", (lambda e, fv=fv, sv16=sv16: e.scalar_tensor_tensor(
                out=fv[:, 1:8:2, :], in0=sv16, scalar=-256.0 / 255.0,
                in1=fv[:, 1:8:2, :], op0=ALU.mult, op1=ALU.add)))
        t_fb = S.add("vector", COPY(a_fT, a_feat), sem="dv", inc=1)
        t_ffm = None
        for tc in range(4):
            slot = tc % 4
            pt = p_c.bitcast(BF16)[0:64, slot * 128:slot * 128 + 128]
            t_t = S.add("tensor", TRANS(pt, a_fT[:, tc * 64:(tc + 1) * 64], cn["id_bf"]),
                        waits=[t_fb, X.get("ptrl"), X.get(("ptr", slot)), X.get("pc_free")], sem="mm", inc=1)
            t_ffm = S.add("vector", COPY(sb["ffm"][0:64, tc * 128:(tc + 1) * 128],
                                         pt[0:64, 0:128]), waits=[t_t], sem="dv", inc=1)
            X[("ptr", slot)] = t_ffm
            X["ptrl"] = t_ffm

        if aparts == "feats":
            continue
        t_rw = S.add("sync", DMA(arena[:, 0:9 * 2176].rearrange("p (k f) -> p k f", k=9),
                                 dwa["rw1"].rearrange("k p f -> p k f")),
                     waits=[X.get(("wfree", 0)), X.get(("wfree", 1))], sem="rw0", inc=16)
        t_l2 = None
        t_l2k = {}
        h1tok = {}
        for kf in range(17):
            par = kf % 2
            t_h1 = None
            for k in range(9):
                rhs = a_xn[:, k * TA:(k + 1) * TA] if k < 8 else sb["ffm"][0:64, 0:512]
                lhsT = arena[:, k * 2176 + kf * 128:k * 2176 + kf * 128 + 128]
                lh = lhsT if k < 8 else lhsT[0:64, :]
                t_h1 = S.add("tensor", MM(p_a[:, 0:512], lh, rhs, k == 0, k == 8),
                             waits=([t_rw, X.get("pa_free"), t_ffm, t_l2k.get(kf - 2)] if k == 0 else ()),
                             sem="mm", inc=(1 if k == 8 else 0))
            t_h1e = S.add("scalar", ACTD(a_h1[:, par * 512:(par + 1) * 512], p_a[:, 0:512], AF.Silu,
                                         bias=rb1[:, kf:kf + 1]),
                          waits=[t_h1], sem="act", inc=1)
            if dbg is not None and blk == 0 and kf < 2:
                S.add("gpsimd", DMA(dbg["h1_%d" % kf], a_h1[:, par * 512:(par + 1) * 512]),
                      waits=[t_h1e], sem="dbg", inc=16)
                S.add("scalar", NOP(), waits=[S.tok("dbg")], sem="act", inc=1)
            X["pa_free"] = t_h1e
            for tc in range(4):
                t_l2 = S.add("tensor", MM(p_d2[:, 448 + tc * 16:448 + tc * 16 + 16],
                                          a_h1[:, par * 512 + tc * 128:par * 512 + tc * 128 + 128],
                                          rw2[:, kf * 16:(kf + 1) * 16], True, True),
                             waits=([t_h1e, X.get("pd2o"), X.get("pd2_free"), X.get("l2ev")]
                                    if tc == 0 else [t_h1e]),
                             sem="mm", inc=(1 if tc == 3 else 0))
            if kf == 0:
                X["l2ev"] = S.add("vector", COPY(sb["l2acc"], p_d2[:, 448:512]),
                                  waits=[t_l2], sem="dv", inc=1)
            else:
                X["l2ev"] = S.add("vector", TT(sb["l2acc"], sb["l2acc"], p_d2[:, 448:512], ALU.add),
                                  waits=[t_l2], sem="dv", inc=1)
            t_l2k[kf] = X["l2ev"]
        X[("wfree", 0)] = S.tok("mm")
        X[("wfree", 1)] = S.tok("mm")
        if dbg is not None and blk == 0:
            S.add("gpsimd", DMA(dbg["l2a"], sb["l2acc"]), waits=[X.get("l2ev")], sem="dbg", inc=16)
            S.add("vector", NOP(), waits=[S.tok("dbg")], sem="dv", inc=1)
        l2 = a_sc[:, 0:64]
        t_l2e = S.add("vector", TT(l2, sb["l2acc"], rb2r, ALU.add),
                      waits=[X.get("l2ev"), X.get("scact")], sem="dv", inc=1)
        X["pd2o"] = t_l2e
        X["pd2_free"] = t_l2e
        l23 = l2.rearrange("p (s i) -> p s i", i=4)
        mx4 = a_sc[:, 64:80]
        mx43 = mx4.rearrange("p (s one) -> p s one", one=1)
        S.add("vector", RED(mx43, l23, ALU.max))
        t_sub = S.add("vector", TT(l23, l23, mx43.broadcast_to([128, 16, 4]), ALU.subtract), sem="dv", inc=1)
        t_exp = S.add("scalar", ACTF(l2, l2, AF.Exp), waits=[t_sub], sem="act", inc=1)
        S.add("vector", RED(mx43, l23, ALU.add), waits=[t_exp])
        S.add("vector", RECIP(a_sc[:, 80:96], mx4))
        S.add("vector", TT(l23, l23, a_sc[:, 80:96].rearrange("p (s one) -> p s one", one=1).broadcast_to([128, 16, 4]), ALU.mult))
        t_p = S.add("vector", TS(a_ptm, l2, 0.96, 0.01, ALU.mult, ALU.add), sem="dv", inc=1)
        X["scact"] = t_p

        if aparts == "rtr":
            continue
        t_on = {}
        t_onf = {}
        for tc in range(4):
            mxv = a_mx.bitcast(BF16)[:, 0:1024]
            m3 = mxv.rearrange("p (h d) -> p h d", d=256)
            p4 = a_ptm[:, tc * 16:(tc + 1) * 16].rearrange("p (h i) -> p h i", i=4)
            om = lambda kk: otm[:, kk * 4096 + tc * 1024:kk * 4096 + tc * 1024 + 1024]
            o3m = lambda kk: om(kk).rearrange("p (h d) -> p h d", d=256)
            S.add("vector", TT(m3, o3m(0), p4[:, :, 0:1].broadcast_to([128, 4, 256]), ALU.mult),
                  waits=[t_p, X.get("mixact")])
            for i in range(1, 4):
                scr3 = a_sc.bitcast(BF16)[:, 0:1024].rearrange("p (h d) -> p h d", d=256)
                S.add("vector", TT(scr3, o3m(i), p4[:, :, i:i + 1].broadcast_to([128, 4, 256]), ALU.mult))
                S.add("vector", TT(mxv, mxv, a_sc.bitcast(BF16)[:, 0:1024], ALU.add))
            for rnd in range(2):
                sq = a_sc.bitcast(BF16)[:, 0:1024]
                S.add("vector", TT(sq, mxv, mxv, ALU.mult))
                t_red = S.add("vector", RED(a_mx[:, 1024:1028].rearrange("p (h one) -> p h one", one=1),
                                            sq.rearrange("p (h d) -> p h d", d=256), ALU.add), sem="dv", inc=1)
                t_rms = S.add("scalar", ACTF(a_mx[:, 1028:1032], a_mx[:, 1024:1028], AF.Sqrt,
                                             bias=cn["eps"][0:128, 0:1], scale=1.0 / 256.0),
                              waits=[t_red], sem="act", inc=1)
                S.add("vector", RECIP(a_mx[:, 1024:1028], a_mx[:, 1028:1032]), waits=[t_rms])
                S.add("vector", TT(m3, m3, a_mx[:, 1024:1028].rearrange("p (h one) -> p h one", one=1).broadcast_to([128, 4, 256]), ALU.mult))
                if rnd == 0:
                    S.add("vector", TT(mxv, mxv, mixw, ALU.mult))
            t_onx = S.add("vector", TT(a_on[:, (tc % 2) * 1024:(tc % 2) * 1024 + 1024], mxv, onormw, ALU.mult),
                          waits=[X.get(("onsl", tc % 2))], sem="dv", inc=1)
            t_on[tc] = t_onx
            X["mixact"] = t_onx
            for c8 in range(8):
                slot = c8 % 4
                pt = p_c.bitcast(BF16)[:, slot * 128:slot * 128 + 128]
                t_t = S.add("tensor", TRANS(pt, a_on[:, (tc % 2) * 1024 + c8 * 128:(tc % 2) * 1024 + c8 * 128 + 128],
                                            cn["id_bf"]),
                            waits=[t_onx, X.get("ptrl"), X.get(("ptr", slot))], sem="mm", inc=1)
                t_e = S.add("vector", COPY(a_onf[:, c8 * TA + tc * 128:c8 * TA + tc * 128 + 128], pt),
                            waits=[t_t], sem="dv", inc=1)
                X[("ptr", slot)] = t_e
                X["ptrl"] = t_e
                t_onf[(tc, c8)] = t_e
            X[("onsl", tc % 2)] = S.tok("mm")
        X["scact"] = t_on[3]
        X["pc_free"] = S.tok("dv")
        ld_w(3)
        for c in range(8):
            t_wo = None
            for k in range(8):
                wsl = arena[:, 8192 + k * 1024 + c * 128:8192 + k * 1024 + c * 128 + 128]
                t_wo = S.add("tensor", MM(p_b2[:, 0:512], wsl, a_onf[:, k * TA:(k + 1) * TA],
                                          k == 0, k == 7),
                             waits=([wtok[3], X.get("pb2_free")] + [t_onf[(tc2, k)] for tc2 in range(4)]
                                    if k == 0 else [t_onf[(tc2, k)] for tc2 in range(4)]),
                             sem="mm", inc=(1 if k == 7 else 0))
            par = c % 2
            t_se2 = S.add("vector", COPY(a_sout[:, par * 512:(par + 1) * 512], p_b2[:, 0:512]),
                          waits=[t_wo, X.get(("sof", par))], sem="dv", inc=1)
            X["pb2_free"] = t_se2
            t_sd = S.add("scalar", DMA(d_s[c, :, t0:t0 + TA], a_sout[:, par * 512:(par + 1) * 512]),
                         waits=[t_se2], sem="so%d" % par, inc=16)
            X[("sof", par)] = t_sd
        X[("wfree", 1)] = S.tok("mm")
    if dbg is not None and aparts == "all":
        w = [X[("sof", 0)], X[("sof", 1)]]
        for nm, tile in [("xn", a_xn), ("q", a_q), ("k", a_k), ("v", a_v),
                         ("beta", beta_tm), ("otm", otm), ("feat", a_feat),
                         ("ptm", a_ptm), ("onf", a_onf), ("sbf", a_sbf), ("ffm", sb["ffm"])]:
            S.add("gpsimd", DMA(dbg[nm], tile), waits=w, sem="dbg", inc=16)
    return


def emit_convert(S, d_in, dws, ne):
    for e in range(ne):
        for m, nm in [(0, "w_gate"), (1, "w_up")]:
            src = d_in[nm][e]
            for g in range(8):
                for k in range(8):
                    sap = src.rearrange("(kk p) (gg i f) -> kk gg p i f", p=128, i=4, f=128)[k, g]
                    S.add("gpsimd", DMA(dws["wgu"][e, m, g, :, k], sap), sem="cvt", inc=16)
        src = d_in["w_down"][e]
        for j in range(8):
            sap = src.rearrange("(i p) (jj c) -> jj p i c", p=128, c=128)[j]
            S.add("gpsimd", DMA(dws["wd"][e, j], sap), sem="cvt", inc=16)


def build_nc(cfg):
    nblk, ne, use_s = cfg["nblk"], cfg["ne"], cfg["use_s"]
    use_attn = cfg.get("use_attn", False)
    nc = bass.Bass("TRN2", target_bir_lowering=False)

    d_in = {}
    d_in["x"] = nc.dram_tensor("x", [D, L], F32, kind="ExternalInput").ap()
    for nm, shp in [("w_gate", [NE, D, DFF]), ("w_up", [NE, D, DFF]), ("w_down", [NE, DFF, D]),
                    ("router_wT", [D, NE]), ("wqkvo", [4, D, D]), ("rw1p", [1152, 2176]),
                    ("rw2p", [2176, 16]), ("wbr", [8, 128, 4]), ("cvwr", [128, 8 * 44]),
                    ("rb1r", [128, 17]), ("rb2r", [128, 64]),
                    ("mixwr", [128, 1024]), ("onormr", [128, 1024])]:
        d_in[nm] = nc.dram_tensor(nm, shp, F32, kind="ExternalInput").ap()
    d_c = {}
    for nm, shp, dt in [("ones_bf", [128, 1], BF16), ("ones_row", [1, 128], BF16),
                        ("eps", [128, 2], F32), ("iota8", [128, 64], F32),
                        ("iota8r", [128, 64], F32), ("E8", [8, NE * 128], BF16),
                        ("id_bf", [128, 128], BF16), ("mkA", [128, 128], BF16),
                        ("mkAu", [128, 128], BF16), ("mkT", [128, 128], BF16),
                        ("mD16u", [128, 128], BF16), ("mD16l", [128, 128], BF16),
                        ("mB16u", [128, 128], BF16), ("mB16l", [128, 128], BF16),
                        ("mB32u", [128, 128], BF16), ("mB32l", [128, 128], BF16),
                        ("mB64u", [128, 128], BF16), ("mB64l", [128, 128], BF16),
                        ("fscale", [128, 256], F32)]:
        d_c[nm] = nc.dram_tensor("c_" + nm, shp, dt, kind="ExternalInput").ap()

    d_out = nc.dram_tensor("out", [8, 128, L], F32, kind="ExternalOutput").ap()
    d_s = nc.dram_tensor("s_scr", [8, 128, L], F32,
                         kind=("ExternalOutput" if cfg.get("dbg_s") else "Internal")).ap()

    dws = {
        "wgu": nc.dram_tensor("wgu_s", [NE, 2, 8, 128, 8, 4, 128], BF16, kind="Internal").ap(),
        "wd": nc.dram_tensor("wd_s", [NE, 8, 128, 32, 128], BF16, kind="Internal").ap(),
        "cwscr": nc.dram_tensor("cw_scr", [64, 128], BF16, kind="Internal").ap(),
    }
    dws2_outer = {
        "wqkv": nc.dram_tensor("wqkv_s", [4, 128, 8, 1024], BF16, kind="Internal").ap(),
        "rw1": nc.dram_tensor("rw1_s", [9, 128, 2176], BF16, kind="Internal").ap(),
    }

    cnt = {}
    with contextlib.ExitStack() as es:
        def sbt(name, shape, dt):
            return es.enter_context(nc.sbuf_tensor(name, shape, dt)).ap()
        sems = {k: es.enter_context(nc.semaphore(k))
                for k in ["ld", "cvt", "mm", "dv", "act", "fin", "xa", "xv", "sv", "cwd",
                          "wg0", "wg1", "wu0", "wu1", "wd0", "wd1", "st0", "st1",
                          "wq0", "wq1", "rw0", "so0", "so1", "gp", "dbg"]}
        cn = {"ones_bf": sbt("k_ones_bf", [128, 1], BF16),
              "eps": sbt("k_eps", [128, 2], F32),
              "ones_row": sbt("k_ones_row", [1, 128], BF16),
              "iota8": sbt("k_iota8", [128, 64], F32),
              "iota8r": sbt("k_iota8r", [128, 64], F32),
              "id_bf": sbt("k_id_bf", [128, 128], BF16),
              "mkA": sbt("k_mkA", [128, 128], BF16),
              "mkAu": sbt("k_mkAu", [128, 128], BF16),
              "mkT": sbt("k_mkT", [128, 128], BF16),
              "mD16u": sbt("k_mD16u", [128, 128], BF16),
              "mD16l": sbt("k_mD16l", [128, 128], BF16),
              "mB16u": sbt("k_mB16u", [128, 128], BF16),
              "mB16l": sbt("k_mB16l", [128, 128], BF16),
              "mB32u": sbt("k_mB32u", [128, 128], BF16),
              "mB32l": sbt("k_mB32l", [128, 128], BF16),
              "mB64u": sbt("k_mB64u", [128, 128], BF16),
              "mB64l": sbt("k_mB64l", [128, 128], BF16),
              "fscale": sbt("k_fscale", [128, 256], F32),
              "E8": sbt("k_E8", [8, NE * 128], BF16)}
        s_rw = sbt("s_rw", [128, 8 * NE], BF16)
        s_rw2 = sbt("s_rw2", [128, 17 * 16], BF16)
        s_wbr = sbt("s_wbr", [128, 8 * 4], BF16)
        s_cvw = sbt("s_cvw", [128, 8 * 44], F32)
        s_rb1 = sbt("s_rb1", [128, 17], F32)
        s_rb2 = sbt("s_rb2", [128, 64], F32)
        s_mixw = sbt("s_mixw", [128, 1024], BF16)
        s_onorm = sbt("s_onorm", [128, 1024], BF16)
        dws2 = dws2_outer

        S0 = Sched(cnt)
        for nm in cn:
            S0.add("sync", DMA(cn[nm], d_c[nm]), sem="ld", inc=16)
        S0.add("gpsimd", DMA(s_rw.rearrange("p (k e) -> p k e", k=8),
                             d_in["router_wT"].rearrange("(k p) e -> p k e", p=128)), sem="cvt", inc=16)
        emit_convert(S0, d_in, dws, ne)
        if use_attn:
            for m in range(4):
                src = d_in["wqkvo"][m]
                for k in range(8):
                    S0.add("gpsimd", DMA(dws2["wqkv"][m, :, k],
                                         src.rearrange("(k p) f -> k p f", p=128)[k]),
                           sem="cvt", inc=16)
            for k in range(9):
                S0.add("gpsimd", DMA(dws2["rw1"][k],
                                     d_in["rw1p"].rearrange("(k p) f -> k p f", p=128)[k]),
                       sem="cvt", inc=16)
            S0.add("gpsimd", DMA(s_rw2.rearrange("p (k e) -> p k e", e=16),
                                 d_in["rw2p"].rearrange("(k p) e -> p k e", p=128)), sem="cvt", inc=16)
            S0.add("gpsimd", DMA(s_wbr.rearrange("p (k e) -> p k e", e=4),
                                 d_in["wbr"].rearrange("k p e -> p k e")), sem="cvt", inc=16)
            S0.add("gpsimd", DMA(s_mixw, d_in["mixwr"]), sem="cvt", inc=16)
            S0.add("gpsimd", DMA(s_onorm, d_in["onormr"]), sem="cvt", inc=16)
            S0.add("sync", DMA(s_cvw, d_in["cvwr"]), sem="ld", inc=16)
            S0.add("sync", DMA(s_rb1, d_in["rb1r"]), sem="ld", inc=16)
            S0.add("sync", DMA(s_rb2, d_in["rb2r"]), sem="ld", inc=16)
        S0.add("gpsimd", NOP(), waits=[("cvt", cnt.get("cvt", 0))], sem="fin", inc=1)
        S0.add("sync", NOP(), waits=[("ld", cnt.get("ld", 0))], sem="fin", inc=1)
        with nc.Block() as blk0:
            S0.emit(blk0, sems)

        if use_attn:
            with contextlib.ExitStack() as esa:
                def sbta(name, shape, dt):
                    return esa.enter_context(nc.sbuf_tensor(name, shape, dt)).ap()
                sba = {
                    "x": sbta("a_x", [128, 8 * TA], BF16),
                    "xn": sbta("a_xn", [128, 8 * TA], BF16),
                    "z": sbta("a_z", [128, 2 * 515], F32),
                    "cs": sbta("a_cs", [128, 512], F32),
                    "csp": sbta("a_csp", [128, 512], F32),
                    "q": sbta("a_q", [128, 8 * TA], BF16),
                    "k": sbta("a_k", [128, 8 * TA], BF16),
                    "v": sbta("a_v", [128, 8 * 640], BF16),
                    "lo": sbta("a_lo", [128, 2 * 512], BF16),
                    "mi": sbta("a_mi", [128, 2 * 512], BF16),
                    "otm": sbta("a_otm", [128, 4 * 4096], BF16),
                    "feat": sbta("a_feat", [128, 256], F32),
                    "fT": sbta("a_fT", [128, 256], BF16),
                    "ffm": sbta("a_ffm", [64, 512], BF16),
                    "ptm": sbta("a_ptm", [128, 64], F32),
                    "h1": sbta("a_h1", [128, 2 * 512], BF16),
                    "mx": sbta("a_mx", [128, 1040], F32),
                    "on": sbta("a_on", [128, 2 * 1024], BF16),
                    "onf": sbta("a_onf", [128, 8 * TA], BF16),
                    "stat": sbta("a_stat", [1, 2 * TA], F32),
                    "statr": sbta("a_statr", [1, TA], BF16),
                    "tl": sbta("a_tl", [128, 72 + 192], F32),
                    "ch": sbta("a_ch", [128, 12 * 512], BF16),
                    "sc": sbta("a_sc", [128, 1024], F32),
                    "sbf": sbta("a_sbf", [128, 4 * 512], BF16),
                    "dt": sbta("a_dt", [128, 2080], BF16),
                    "dt2": sbta("a_dt2", [128, 2 * 1024], BF16),
                    "mb": sbta("a_mb", [128, 512], BF16),
                    "xm": sbta("a_xm", [128, 768], BF16),
                    "row": sbta("a_row", [1, 768], BF16),
                    "sc2": sbta("a_sc2", [128, 12], F32),
                    "l2acc": sbta("a_l2acc", [128, 64], F32),
                    "beta": sbta("a_beta", [128, 16], F32),
                    "arena": sbta("a_arena", [128, 9 * 2176], BF16),
                    "rw2": s_rw2, "wb": s_wbr, "cvw": s_cvw,
                    "rb1": s_rb1, "rb2": s_rb2, "mixw": s_mixw, "onorm": s_onorm,
                }
                psa = {k: esa.enter_context(nc.psum_tensor("pa_" + k, [128, 512], F32)).ap()
                       for k in ["a", "b2", "c", "d2", "e", "f", "g2", "h2"]}
                sba["sf32"] = sbta("a_sf32", [128, 2048], F32)
                SA = Sched(cnt)
                SA.add("tensor", NOP(), waits=[("fin", 2)])
                SA.add("sync", NOP(), waits=[("fin", 2)])
                SA.add("vector", NOP(), waits=[("fin", 2)], sem="dv", inc=1)
                SA.add("gpsimd", NOP(), waits=[("fin", 2)])
                dbg = None
                if cfg.get("dbg_s"):
                    dbg = {}
                    for nm, shp, dt in [("vd0", [128, 4096], BF16),
                                    ("row", [1, 384], BF16), ("xm", [128, 384], BF16),
                                    ("l2a", [128, 64], F32),
                                    ("h1_0", [128, 512], BF16), ("h1_1", [128, 512], BF16),
                                    ("wt", [128, 256], BF16), ("up", [128, 256], BF16),
                                    ("sb0", [128, 512], BF16),
                                    ("ch0", [128, 512], BF16), ("ch1", [128, 512], BF16),
                                    ("ch8", [128, 512], BF16), ("dt2", [128, 2048], BF16),
                                    ("sc2", [128, 8], F32), ("M", [128, 512], BF16),
                                    ("xn", [128, 4096], BF16), ("q", [128, 4096], BF16),
                                        ("k", [128, 4096], BF16), ("v", [128, 8 * 640], BF16),
                                        ("beta", [128, 16], F32), ("otm", [128, 16384], BF16),
                                        ("feat", [128, 256], F32), ("ptm", [128, 64], F32),
                                        ("onf", [128, 4096], BF16), ("sbf", [128, 2048], BF16),
                                        ("ffm", [64, 512], BF16)]:
                        dbg[nm] = nc.dram_tensor("dbg_" + nm, shp, dt, kind="ExternalOutput").ap()
                emit_attn(SA, d_in["x"], d_s, dws2_outer, sba, psa, cn, nblka=cfg.get("nblka", NBLKA),
                          dbg=dbg,
                          aparts=cfg.get("aparts", "all"))
                SA.add("gpsimd", NOP(), waits=[("so0", cnt.get("so0", 0)), ("so1", cnt.get("so1", 0)),
                                               ("dbg", cnt.get("dbg", 0))],
                       sem="fin", inc=1)
                with nc.Block() as blka:
                    SA.emit(blka, sems)

        with contextlib.ExitStack() as es1:
            def sbt1(name, shape, dt):
                return es1.enter_context(nc.sbuf_tensor(name, shape, dt)).ap()
            sb = {"xn": sbt1("b_xn", [128, 8 * TB], BF16),
                  "xnup": sbt1("b_xnup", [128, 8 * TB], BF16),
                  "h": sbt1("b_h", [128, 32 * TB], BF16),
                  "acc": sbt1("b_acc", [128, 8 * TB], F32),
                  "wg": sbt1("b_wg", [128, 2 * 8 * 4 * 128], BF16),
                  "wu": sbt1("b_wu", [128, 2 * 8 * 4 * 128], BF16),
                  "wd": sbt1("b_wd", [128, 2 * 32 * 128], BF16),
                  "sg": sbt1("b_sg", [128, 1024], F32),
                  "rw": s_rw,
                  "stat": sbt1("b_stat", [97, TB], F32),
                  "statr": sbt1("b_statr", [1, TB], BF16),
                  "ltm": sbt1("b_ltm", [128, 64], F32),
                  "top": sbt1("b_top", [128, 256], F32),
                  "cw": sbt1("b_cw", [8, TB], BF16),
                  "msk1": sbt1("b_msk1", [128, 64], F32),
                  "msk2": sbt1("b_msk2", [128, 64], F32),
                  "mskd": sbt1("b_mskd", [128, 64], F32),
                  "t1m": sbt1("b_t1m", [128, 64], F32),
                  "cwtm": sbt1("b_cwtm", [128, 64], F32),
                  "cwr": sbt1("b_cwr", [128, 64], BF16),
                  "cwT": sbt1("b_cwT", [64, 128], BF16)}
            psb = {k: es1.enter_context(nc.psum_tensor("p_" + k, [128, 1024], F32)).ap()
                   for k in ["g", "u", "d", "b"]}

            S1 = Sched(cnt)
            if cfg.get("skip_moe"):
                S1.add("gpsimd", NOP(), waits=[("fin", 3)], sem="fin", inc=1)
                with nc.Block() as blk1:
                    S1.emit(blk1, sems)
                return nc
            fin_need = 3 if use_attn else 2
            S1.add("tensor", NOP(), waits=[("fin", fin_need)])
            S1.add("sync", NOP(), waits=[("fin", fin_need)])
            emit_moe(S1, d_in["x"], d_out, d_s if use_s else None, dws, sb, psb, cn, use_s, nblk, ne, parts=cfg.get("parts", "all"))
            S1.add("gpsimd", NOP(), waits=[("st0", cnt.get("st0", 0)), ("st1", cnt.get("st1", 0))],
                   sem="fin", inc=1)
            with nc.Block() as blk1:
                S1.emit(blk1, sems)
    return nc


def _msk(f):
    i, j = np.mgrid[0:128, 0:128]
    return f(i, j).astype(np.float32).astype(BF)


def _consts_host():
    ONES = np.ones((128, 128), np.float32)
    fs = np.zeros(64, np.float32)
    for f in range(14):
        v = 1.0 / 255.0 if (f < 8 and f % 2 == 1) else 1.0 / 256.0
        fs[f * 4:(f + 1) * 4] = v
    return {
        "c_mkA": (-np.tril(ONES, -1)).astype(BF),
        "c_mkAu": (-np.triu(ONES, 1)).astype(BF),
        "c_mkT": np.triu(ONES, 0).astype(BF),
        "c_mD16u": _msk(lambda i, j: (j > i) & (i // 16 == j // 16)),
        "c_mD16l": _msk(lambda i, j: (i > j) & (i // 16 == j // 16)),
        "c_mB16u": _msk(lambda i, j: (j // 16 == i // 16 + 1) & (i // 32 == j // 32)),
        "c_mB16l": _msk(lambda i, j: (i // 16 == j // 16 + 1) & (i // 32 == j // 32)),
        "c_mB32u": _msk(lambda i, j: (j // 32 == i // 32 + 1) & (i // 64 == j // 64)),
        "c_mB32l": _msk(lambda i, j: (i // 32 == j // 32 + 1) & (i // 64 == j // 64)),
        "c_mB64u": _msk(lambda i, j: (j // 64 == i // 64 + 1)),
        "c_mB64l": _msk(lambda i, j: (i // 64 == j // 64 + 1)),
        "c_fscale": np.tile(fs, (128, 4)),
        "c_ones_bf": np.ones((128, 1), np.float32).astype(BF),
        "c_iota8": np.tile(np.arange(8, dtype=np.float32), (128, 8)),
        "c_iota8r": np.tile(np.arange(7, -1, -1, dtype=np.float32), (128, 8)),
        "c_eps": np.tile(np.array([1e-5, 1e-12], np.float32), (128, 1)),
        "c_id_bf": np.eye(128, dtype=np.float32).astype(BF),
        "c_ones_row": np.ones((1, 128), np.float32).astype(BF),

        "c_E8": np.repeat(np.eye(8, dtype=np.float32), 128, axis=1).astype(BF),
    }


def make_in_maps(inputs):
    x = np.asarray(inputs["L0_attn_hidden_states"])
    base = {
        "w_gate": np.ascontiguousarray(np.asarray(inputs["L0_ffn_w_gate"], np.float32)),
        "w_up": np.ascontiguousarray(np.asarray(inputs["L0_ffn_w_up"], np.float32)),
        "w_down": np.ascontiguousarray(np.asarray(inputs["L0_ffn_w_down"], np.float32)),
        "router_wT": np.ascontiguousarray(np.asarray(inputs["L0_ffn_router_w"], np.float32).T),
    }
    g32 = lambda k, shp: np.asarray(inputs[k], np.float32) if k in inputs else np.zeros(shp, np.float32)
    wq = g32("L0_attn_Wq", (D, D)); wk = g32("L0_attn_Wk", (D, D))
    wv = g32("L0_attn_Wv", (D, D)); wo = g32("L0_attn_Wo", (D, D))
    base["wqkvo"] = np.ascontiguousarray(np.stack([wq, wk, wv, wo], 0))
    rw1 = g32("L0_attn_r_w1", (1080, 2160))
    rw1p = np.zeros((1152, 2176), np.float32); rw1p[:1080, :2160] = rw1
    base["rw1p"] = rw1p
    rw2 = g32("L0_attn_r_w2", (2160, 16))
    rw2p = np.zeros((2176, 16), np.float32); rw2p[:2160] = rw2
    base["rw2p"] = rw2p
    base["wbr"] = np.ascontiguousarray(g32("L0_attn_Wb", (D, H)).reshape(8, 128, 4))
    cv = np.zeros((128, 8 * 44), np.float32)
    cq = g32("L0_attn_conv_q_w", (D, 4)).reshape(8, 128, 4)
    ck = g32("L0_attn_conv_k_w", (D, 4)).reshape(8, 128, 4)
    cvv = g32("L0_attn_conv_v_w", (D, 4)).reshape(8, 128, 4)
    lw = g32("L0_attn_local_w", (D, 7)).reshape(8, 128, 7)
    mw = g32("L0_attn_mid_w", (D, 25)).reshape(8, 128, 25)
    for c in range(8):
        cv[:, c * 44 + 0:c * 44 + 4] = cq[c]
        cv[:, c * 44 + 4:c * 44 + 8] = ck[c]
        cv[:, c * 44 + 8:c * 44 + 12] = cvv[c]
        cv[:, c * 44 + 12:c * 44 + 19] = lw[c]
        cv[:, c * 44 + 19:c * 44 + 44] = mw[c]
    base["cvwr"] = cv
    rb1 = g32("L0_attn_r_b1", (2160,))
    rb1p = np.zeros(2176, np.float32); rb1p[:2160] = rb1
    base["rb1r"] = np.ascontiguousarray(rb1p.reshape(17, 128).T)
    base["rb2r"] = np.tile(g32("L0_attn_r_b2", (16,)), (128, 4))
    base["mixwr"] = np.tile(g32("L0_attn_mix_w", (H, DH)).reshape(-1), (128, 1))
    base["onormr"] = np.tile(np.tile(g32("L0_attn_onorm_w", (DH,)), H), (128, 1))
    base.update(_consts_host())
    in_maps = []
    for c in range(NCORES):
        m = dict(base)
        m["x"] = np.ascontiguousarray(np.asarray(x[c], np.float32).T)
        in_maps.append(m)
    return in_maps


def kernel(**inputs):
    in_maps = make_in_maps(inputs)
    nc = build_nc(CFG)
    res = run_bass_kernel_spmd(nc, in_maps, core_ids=list(range(NCORES)))
    outs = []
    for c in range(NCORES):
        o = np.asarray(res.results[c]["out"])
        outs.append(o.reshape(D, L).T)
    return np.stack(outs, 0)
